# version 13
# baseline (speedup 1.0000x reference)
import os
import numpy as np
import ml_dtypes
from contextlib import ExitStack

import concourse.bass as bass
import concourse.mybir as mybir
from concourse.bass_utils import run_bass_kernel_spmd
from concourse.bass_types import AP

F32 = mybir.dt.float32
BF16 = mybir.dt.bfloat16
AF = mybir.ActivationFunctionType
ALU = mybir.AluOpType
AX = mybir.AxisListType
BFNP = ml_dtypes.bfloat16

D = 1024
T = 2048
NCH = 8
NTC = 16
NTAU = 4
NH = 16
HS = 64
NE = 8
NSEQ = 2

FFN0A, ATTN0, FFN0B, FFN1A, ATTN1, FFN1B = range(6)
ALL_SUBS = [FFN0A, ATTN0, FFN0B, FFN1A, ATTN1, FFN1B]
EMITTERS = {FFN0A: "ffn0", ATTN0: "attn0", FFN0B: "ffn0",
            FFN1A: "ffn1", ATTN1: "attn1", FFN1B: "ffn1"}

LAUNCH_GROUPS = [ALL_SUBS]

ENGSEM = {"vector": "dve", "scalar": "act", "tensor": "pe", "gpsimd": "pool"}


def _regions(kwargs):
    reads, writes = [], []
    for k, v in kwargs.items():
        if not isinstance(v, AP):
            continue
        try:
            pairs = list(v.ap)
            w = pairs[0][0]
            off = v.offset
            p0 = off // w if w else 0
            c0 = off % w if w else off
            pcnt = pairs[0][1]
            span = 1
            for stp, cn in pairs[1:]:
                span += (cn - 1) * abs(stp)
            reg = (v.tensor.name, p0, p0 + pcnt, c0, c0 + span)
        except Exception:
            reg = (v.tensor.name, 0, 1 << 30, 0, 1 << 30)
        if k in ("out", "accum_out", "ap"):
            writes.append(reg)
        else:
            reads.append(reg)
    return reads, writes


def _overlap(a, b):
    return (a[0] == b[0] and a[1] < b[2] and b[1] < a[2]
            and a[3] < b[4] and b[3] < a[4])


class Prog:
    ENGS = ("sync", "gpsimd", "vector", "scalar", "tensor")

    def __init__(self):
        self.steps = {e: [] for e in self.ENGS}
        self.cnt = {s: 0 for s in
                    ("pe", "dve", "act", "pool", "g", "h",
                     "g0", "g1", "g2", "g3", "g4", "g5")}
        self.maxwait = {}
        self.dirty = {e: [] for e in self.ENGS}
        self.rdirty = {e: [] for e in self.ENGS}

    def wait(self, eng, sem, val):
        if val <= 0:
            return
        if self.maxwait.get((eng, sem), -1) >= val:
            return
        self.maxwait[(eng, sem)] = val
        self.steps[eng].append(("wait", sem, val))

    def op(self, eng, method, inc=None, **kwargs):
        if eng in ("vector", "scalar"):
            reads, writes = _regions(kwargs)
            need = any(_overlap(r, d) for r in reads for d in self.dirty[eng])
            if not need:
                need = any(_overlap(wr, d) for wr in writes
                           for d in self.dirty[eng] + self.rdirty[eng])
            if need:
                self.steps[eng].append(("drain",))
                self.dirty[eng].clear()
                self.rdirty[eng].clear()
            self.dirty[eng].extend(writes)
            self.rdirty[eng].extend(reads)
        incs = []
        if inc is not None:
            sem, amt = inc if isinstance(inc, tuple) else (inc, 1)
            self.cnt[sem] += amt
            incs.append((sem, amt))
        self.steps[eng].append(("op", method, kwargs, incs))
        return self.cnt[incs[0][0]] if incs else None

    def dma(self, eng, out, in_, sem):
        self.cnt[sem] += 16
        self.steps[eng].append(("op", "dma_start", {"out": out, "in_": in_},
                                [(sem, 16)]))
        return self.cnt[sem]

    def replay(self, block, sems):
        for eng_name in self.ENGS:
            steps = self.steps[eng_name]
            if not steps:
                continue

            def body(eng, _steps=steps):
                for st in _steps:
                    if st[0] == "wait":
                        eng.wait_ge(sems[st[1]], st[2])
                    elif st[0] == "drain":
                        eng.drain()
                    else:
                        _, method, kwargs, incs = st
                        try:
                            ins = getattr(eng, method)(**kwargs)
                        except Exception:
                            import sys
                            print("REPLAY FAIL:", eng_name, method,
                                  {k: (str(v.tensor.name) + str(list(v.ap)) +
                                       f"@{v.offset}")
                                   if isinstance(v, AP) else v
                                   for k, v in kwargs.items()},
                                  file=sys.stderr)
                            raise
                        for sem, amt in incs:
                            ins.then_inc(sems[sem], amt)
            getattr(block, eng_name)(body)


class Bank:
    def __init__(self, tile):
        self.t = tile
        self.ready = 0
        self.free = {}
        self.started = False


class Slot:
    def __init__(self, ap, pitch, sem):
        self.ap = ap
        self.pitch = pitch
        self.sem = sem
        self.ready = 0
        self.mark = 0

    def tile(self, c, off=0, wdt=128):
        return self.ap[:, c * self.pitch + off: c * self.pitch + off + wdt]


class Ctx:
    def __init__(self, nc, prog):
        self.nc = nc
        self.p = prog
        self.t = {}
        self.banks = {}
        self.c_ident = None
        self.zv = [0] * NTAU

    def mm(self, bank, out, lhsT, rhs, first, last, pinc=None, tp=None,
           slots=(), waits=()):
        p = self.p
        if first and not bank.started:
            for eng, v in bank.free.items():
                p.wait("tensor", ENGSEM[eng], v)
            bank.free = {}
            bank.started = True
        for s in slots:
            p.wait("tensor", s.sem, s.ready)
        for sem, v in waits:
            p.wait("tensor", sem, v)
        if pinc is None:
            pinc = last
        kw = dict(out=out, lhsT=lhsT, rhs=rhs, start=first, stop=last)
        if tp is not None:
            try:
                lw = list(lhsT.ap)[0][0]
                lbase = lhsT.offset // lw if lw else 0
            except Exception:
                lbase = 0
            kw["tile_position"] = (lbase, tp[1])
        r = p.op("tensor", "matmul", inc=("pe", 1) if pinc else None, **kw)
        if pinc:
            bank.ready = r
            for s in slots:
                s.mark = r
        if last:
            bank.started = False
        return r

    def trans(self, bank, out, in_, first, last, waits=()):
        p = self.p
        if first and not bank.started:
            for eng, v in bank.free.items():
                p.wait("tensor", ENGSEM[eng], v)
            bank.free = {}
            bank.started = True
        for sem, v in waits:
            p.wait("tensor", sem, v)
        r = p.op("tensor", "transpose", inc=("pe", 1) if last else None,
                 out=out, in_=in_, identity=self.c_ident)
        if last:
            bank.ready = r
            bank.started = False
        return r

    def ev(self, eng, method, banks=(), waits=(), inc=True, **kwargs):
        p = self.p
        for b in banks:
            p.wait(eng, "pe", b.ready)
        for sem, v in waits:
            p.wait(eng, sem, v)
        sem = ENGSEM[eng]
        r = p.op(eng, method, inc=(sem, 1) if inc else None, **kwargs)
        for b in banks:
            b.free[eng] = r if r is not None else p.cnt[sem]
        return r

    def zw(self, tau):
        return [("dve", self.zv[tau])]

    def fence(self, eng, others=("pe", "dve", "act")):
        for sem in others:
            if sem == ENGSEM.get(eng):
                continue
            self.p.wait(eng, sem, self.p.cnt[sem])

    def barrier(self):
        for eng in ("vector", "scalar", "tensor"):
            self.fence(eng)
            self.p.wait(eng, "h", self.p.cnt["h"])
            for gs in ("g", "g0", "g1", "g2", "g3", "g4", "g5"):
                self.p.wait(eng, gs, self.p.cnt[gs])

    def load_slice(self, slot, hbm, j0, width):
        self.p.wait("gpsimd", "pe", slot.mark)
        src = hbm[:, j0:j0 + width].rearrange("(c p) n -> p c n", p=128)
        dst = slot.ap.rearrange("p (c n) -> p c n", c=NCH)
        slot.ready = self.p.dma("gpsimd", dst, src, slot.sem)


def fm(a, c, t0, t1):
    return a[:, c * T + t0: c * T + t1]


def tmv(a, tc, j0, j1):
    return a[:, tc * D + j0: tc * D + j1]


def emit_ln(ctx, dst):
    p, t = ctx.p, ctx.t
    xs = t["xs"]
    pA, pB = ctx.banks["pA"], ctx.banks["pB"]
    pS1, pS2 = ctx.banks["pS1"], ctx.banks["pS2"]
    R1 = t["R1"]
    ctx.fence("scalar")
    p.wait("vector", "act", p.cnt["act"])
    xsv = p.cnt["dve"]
    apv = [None] * NTAU
    pemk = [0, 0]

    def stats(tau):
        par = tau % 2
        t0 = tau * 512
        b0 = par * 32
        SUM = R1[b0:b0 + 1, 0:512]
        VAR = R1[b0:b0 + 1, 512:1024]
        MNEG = R1[b0:b0 + 1, 1024:1536]
        TMP = R1[b0:b0 + 1, 1536:2048]
        RBm = t["A2"][0:1, 14336 + par * 1024: 14848 + par * 1024]
        RBr = t["A2"][0:1, 14848 + par * 1024: 15360 + par * 1024]
        for c in range(NCH):
            cp = c % 2
            xbs = t["xsc"][:, cp * 512:(cp + 1) * 512]
            sqs = t["xsc"][:, 1024 + cp * 512: 1024 + (cp + 1) * 512]
            cv = ctx.ev("scalar", "copy",
                        waits=[("pe", pemk[cp]), ("dve", xsv)],
                        out=xbs, in_=fm(xs, c, t0, t0 + 512))
            sv = ctx.ev("scalar", "activation",
                        out=sqs, in_=xbs, func=AF.Square)
            ctx.mm(pA, pA.t[0:1, 0:512], t["onesb"][:, 0:1], xbs,
                   first=(c == 0), last=(c == NCH - 1), pinc=True,
                   waits=[("act", cv)])
            ctx.mm(pB, pB.t[0:1, 0:512], t["onesb"][:, 0:1], sqs,
                   first=(c == 0), last=(c == NCH - 1), pinc=True,
                   waits=[("act", sv)])
            pemk[cp] = p.cnt["pe"]
        sev = ctx.ev("scalar", "copy", banks=[pA], out=SUM,
                     in_=pA.t[0:1, 0:512])
        sev = ctx.ev("scalar", "copy", banks=[pB], out=VAR,
                     in_=pB.t[0:1, 0:512])
        ctx.ev("vector", "tensor_scalar", waits=[("act", sev)],
               out=MNEG, in0=SUM,
               scalar1=-1.0 / D, scalar2=None, op0=ALU.mult)
        ctx.ev("vector", "tensor_scalar", out=VAR, in0=VAR,
               scalar1=1.0 / D, scalar2=None, op0=ALU.mult)
        ctx.ev("vector", "tensor_tensor", out=TMP, in0=MNEG, in1=MNEG,
               op=ALU.mult)
        dv = ctx.ev("vector", "tensor_tensor", out=VAR, in0=VAR, in1=TMP,
                    op=ALU.subtract)
        ctx.ev("scalar", "activation", waits=[("dve", dv)],
               out=TMP, in_=VAR, func=AF.Ln,
               bias=t["epsc"][b0:b0 + 1, 0:1])
        av = ctx.ev("scalar", "activation", out=VAR, in_=TMP, func=AF.Exp,
                    scale=-0.5)
        ctx.ev("vector", "tensor_copy", out=RBm, in_=MNEG)
        bv = ctx.ev("vector", "tensor_copy", waits=[("act", av)],
                    out=RBr, in_=VAR)
        apv[tau] = (RBm, RBr, bv)

    def apply(tau):
        par = tau % 2
        t0 = tau * 512
        RBm, RBr, bv = apv[tau]
        bm = pS1 if par == 0 else pS2
        ctx.mm(bm, bm.t[:, 0:512], t["onesrX"][0:1, :], RBm,
               first=True, last=True, waits=[("dve", bv)])
        ctx.mm(pTf_ln, pTf_ln.t[:, 0:512], t["onesrX"][0:1, :], RBr,
               first=True, last=True)
        rsb = t["A2"][:, 12288 + par * 512: 12800 + par * 512]
        ctx.ev("vector", "tensor_copy", banks=[pTf_ln], out=rsb,
               in_=pTf_ln.t[:, 0:512])
        p.wait("vector", "pe", p.cnt["pe"])
        for c in range(NCH):
            tsb = t["SCRF"][:, (c % 2) * 256:(c % 2) * 256 + 256].bitcast(BF16)
            ctx.ev("vector", "tensor_tensor",
                   banks=[bm] if c == NCH - 1 else (),
                   waits=[("pe", bm.ready)],
                   out=tsb, in0=fm(xs, c, t0, t0 + 512),
                   in1=bm.t[:, 0:512], op=ALU.add)
            ctx.ev("vector", "tensor_tensor",
                   out=fm(dst, c, t0, t0 + 512), in0=tsb,
                   in1=rsb, op=ALU.mult)

    pTf_ln = ctx.banks["pTf"]
    ctx.zv = [0] * NTAU
    for tau in range(NTAU + 1):
        if tau < NTAU:
            stats(tau)
        if tau >= 1:
            apply(tau - 1)
            ctx.zv[tau - 1] = p.cnt["dve"]


def emit_moe_ffn(ctx, w):
    p, t = ctx.p, ctx.t
    xs = t["xs"]
    A1, A2, A3 = t["A1"], t["A2"], t["A3"]
    pA, pB, pC, pD = (ctx.banks[k] for k in ("pA", "pB", "pC", "pD"))
    pS1, pS2 = ctx.banks["pS1"], ctx.banks["pS2"]
    P13 = t["slotP"]
    S2s = t["slotS"]

    def w13_of(e):
        return w["w13"][e] if e < NE else w["sw13"]

    def w2_of(e):
        return w["w2"][e] if e < NE else w["sw2"]

    ctx.load_slice(P13[0], w13_of(0), 0, 256)
    emit_ln(ctx, A1)

    gts = t["gts"]
    ctx.fence("vector")
    lgs = (t["sc8x"], t["sc8y"])
    for tc in range(NTC):
        br = pS1 if tc % 2 == 0 else pS2
        for c in range(NCH):
            ctx.mm(br, br.t[:, 0:NE],
                   fm(A1, c, tc * 128, tc * 128 + 128),
                   t["rwt"][:, c * NE:(c + 1) * NE],
                   first=(c == 0), last=(c == NCH - 1),
                   waits=ctx.zw(tc // 4))
        lg = lgs[tc % 2]
        lv = ctx.ev("scalar", "copy", banks=[br], out=lg,
                    in_=br.t[:, 0:NE])
        mx = t["sc1a"]
        ctx.ev("vector", "reduce_max", waits=[("act", lv)],
               out=mx, in_=lg, axis=AX.X, negate=True)
        ex = t["sc8b"]
        sm = t["sc1b"]
        av = ctx.ev("scalar", "activation", waits=[("dve", p.cnt["dve"])],
                    out=ex, in_=lg, func=AF.Exp, bias=mx[:, 0:1],
                    accum_out=sm[:, 0:1])
        rs = t["sc1a"]
        ctx.ev("vector", "reciprocal", waits=[("act", av)], out=rs, in_=sm)
        sc = t["sc8a"]
        ctx.ev("vector", "tensor_scalar", out=sc, in0=ex,
               scalar1=rs[:, 0:1], scalar2=None, op0=ALU.mult)
        sb = t["sc8z"]
        ctx.ev("vector", "tensor_tensor", out=sb, in0=sc, in1=t["ebbct"],
               op=ALU.add)
        m1 = t["sc1b"]
        ctx.ev("vector", "reduce_max", out=m1, in_=sb, axis=AX.X)
        eq = t["sc8c"]
        ctx.ev("vector", "tensor_scalar", out=eq, in0=sb,
               scalar1=m1[:, 0:1], scalar2=1e30, op0=ALU.is_ge, op1=ALU.mult)
        ctx.ev("vector", "tensor_tensor", out=eq, in0=sb, in1=eq,
               op=ALU.subtract)
        m2 = t["sc1a"]
        ctx.ev("vector", "reduce_max", out=m2, in_=eq, axis=AX.X)
        msk = t["sc8c"]
        ctx.ev("vector", "tensor_scalar", out=msk, in0=sb,
               scalar1=m2[:, 0:1], scalar2=None, op0=ALU.is_ge)
        ctx.ev("vector", "tensor_tensor", out=sc, in0=sc, in1=msk,
               op=ALU.mult)
        ctx.ev("vector", "tensor_scalar", out=gts[:, tc * NE:(tc + 1) * NE],
               in0=sc, scalar1=0.5, scalar2=None, op0=ALU.mult)

    gate_bcs = (A2[:, 0:2048], A2[:, 2048:4096])
    e1s = (A2[:, 8192:8704], A2[:, 8704:9216])
    e3s = (A2[:, 9216:9728], A2[:, 9728:10240])
    grows = (A2[0:1, 14336:16384], A2[0:1, 10240:12288])
    gmark = [0, 0]

    def gates(e):
        par = e % 2
        grow = grows[par]
        gate_bc = gate_bcs[par]
        gdv = p.cnt["dve"]
        for tcq in range(4):
            for k in range(4):
                tc = tcq * 4 + k
                ctx.mm(pS1, pS1.t[0:1, k * 128:(k + 1) * 128],
                       gts[:, tc * NE + e: tc * NE + e + 1],
                       ctx.c_ident, first=True, last=True,
                       pinc=(k == 3), waits=[("dve", gdv)])
            ctx.ev("vector", "tensor_copy", banks=[pS1],
                   out=grow[0:1, tcq * 512:(tcq + 1) * 512],
                   in_=pS1.t[0:1, 0:512])
        gv = p.cnt["dve"]
        for tau in range(NTAU):
            ctx.mm(pS2, pS2.t[:, 0:512], t["onesrX"][0:1, :],
                   grow[0:1, tau * 512:(tau + 1) * 512],
                   first=True, last=True, waits=[("dve", gv)])
            ctx.ev("vector", "tensor_copy", banks=[pS2],
                   waits=[("dve", gmark[par])],
                   out=gate_bc[:, tau * 512:(tau + 1) * 512],
                   in_=pS2.t[:, 0:512])

    gates(0)
    for e in range(NE + 1):
        gate_bc = gate_bcs[e % 2]
        for hc in range(NCH):
            s13 = P13[hc % 2]
            if hc == 4 and e + 1 < NE:
                gates(e + 1)
            if hc + 1 < NCH:
                ctx.load_slice(P13[(hc + 1) % 2], w13_of(e),
                               (hc + 1) * 256, 256)
            elif e < NE:
                ctx.load_slice(P13[0], w13_of(e + 1), 0, 256)
            for tau in range(NTAU):
                t0 = tau * 512
                bh1 = pA if tau % 2 == 0 else pC
                bh3 = pB if tau % 2 == 0 else pD
                for c in range(NCH):
                    ctx.mm(bh1, bh1.t[:, 0:512], s13.tile(c, 0),
                           fm(A1, c, t0, t0 + 512),
                           first=(c == 0), last=(c == NCH - 1), slots=[s13],
                           waits=ctx.zw(tau))
                for c in range(NCH):
                    ctx.mm(bh3, bh3.t[:, 0:512], s13.tile(c, 128),
                           fm(A1, c, t0, t0 + 512),
                           first=(c == 0), last=(c == NCH - 1), slots=[s13],
                           waits=ctx.zw(tau))
                eo = e1s[tau % 2]
                e3 = e3s[tau % 2]
                sv = ctx.ev("scalar", "activation", banks=[bh1],
                            waits=[("dve", p.cnt["dve"])],
                            out=eo, in_=bh1.t[:, 0:512], func=AF.Silu)
                ctx.ev("vector", "tensor_copy", banks=[bh3],
                       out=e3, in_=bh3.t[:, 0:512])
                gdst = A3[:, hc * T + t0: hc * T + t0 + 512]
                ctx.ev("vector", "tensor_tensor", waits=[("act", sv)],
                       out=gdst, in0=eo, in1=e3, op=ALU.mult)
                if e < NE:
                    gmark[e % 2] = ctx.ev(
                        "vector", "tensor_tensor", out=gdst, in0=gdst,
                        in1=gate_bc[:, t0:t0 + 512], op=ALU.mult)
                else:
                    ctx.ev("vector", "tensor_scalar", out=gdst, in0=gdst,
                           scalar1=0.5, scalar2=None, op0=ALU.mult)
        gready = p.cnt["dve"]
        for dc in range(NCH):
            s2 = S2s[dc % 2]
            if dc == 0:
                ctx.load_slice(s2, w2_of(e), 0, 128)
            if dc + 1 < NCH:
                ctx.load_slice(S2s[(dc + 1) % 2], w2_of(e),
                               (dc + 1) * 128, 128)
            for tau in range(NTAU):
                t0 = tau * 512
                by = pS1 if tau % 2 == 0 else pS2
                for hc in range(NCH):
                    ctx.mm(by, by.t[:, 0:512], s2.tile(hc),
                           A3[:, hc * T + t0: hc * T + t0 + 512],
                           first=(hc == 0), last=(hc == NCH - 1),
                           slots=[s2], waits=[("dve", gready)])
                ctx.ev("vector", "tensor_tensor", banks=[by],
                       out=fm(xs, dc, t0, t0 + 512),
                       in0=fm(xs, dc, t0, t0 + 512),
                       in1=by.t[:, 0:512], op=ALU.add)


def emit_l1ffn(ctx, w):
    p, t = ctx.p, ctx.t
    xs = t["xs"]
    A1 = t["A1"]
    pA, pB = ctx.banks["pA"], ctx.banks["pB"]
    S = t["slotS"]
    ctx.load_slice(S[0], w["cbt"], 0, 128)
    emit_ln(ctx, A1)
    for nc_ in range(NCH):
        s0 = S[nc_ % 2]
        if nc_ + 1 < NCH:
            ctx.load_slice(S[(nc_ + 1) % 2], w["cbt"], (nc_ + 1) * 128, 128)
        for tau in range(NTAU):
            t0 = tau * 512
            b = pA if tau % 2 == 0 else pB
            for c in range(NCH):
                ctx.mm(b, b.t[:, 0:512], s0.tile(c),
                       fm(A1, c, t0, t0 + 512),
                       first=(c == 0), last=(c == NCH - 1), slots=[s0],
                       waits=ctx.zw(tau))
            tscr = (t["SCRF"][:, 0:512] if tau % 2 == 0
                    else t["SCRF"][:, 512:1024])
            av = ctx.ev("scalar", "activation", banks=[b],
                        waits=[("dve", p.cnt["dve"])],
                        out=tscr, in_=b.t[:, 0:512], func=AF.Identity,
                        bias=t["l1bh"][:, nc_:nc_ + 1],
                        scale=t["scbh"][:, nc_:nc_ + 1])
            ctx.ev("vector", "tensor_tensor", waits=[("act", av)],
                   out=fm(xs, nc_, t0, t0 + 512),
                   in0=fm(xs, nc_, t0, t0 + 512), in1=tscr, op=ALU.add)


def emit_attn1(ctx, w):
    p, t = ctx.p, ctx.t
    xs = t["xs"]
    A1, A2, A3, A4 = t["A1"], t["A2"], t["A3"], t["A4"]
    pA, pB, pC, pD = (ctx.banks[k] for k in ("pA", "pB", "pC", "pD"))
    pS1, pS2 = ctx.banks["pS1"], ctx.banks["pS2"]
    pTf = ctx.banks["pTf"]
    R1 = t["R1"]
    P13 = t["slotP"]
    S = t["slotS"]

    ctx.load_slice(S[0], w["wv"], 0, 128)
    emit_ln(ctx, A1)

    for jb in range(NCH):
        s0 = S[jb % 2]
        if jb + 1 < NCH:
            ctx.load_slice(S[(jb + 1) % 2], w["wv"], (jb + 1) * 128, 128)
        elif True:
            ctx.load_slice(S[2], w["wo"], 0, 128)
        for tc in range(NTC):
            b = pA if tc % 2 == 0 else pB
            for c in range(NCH):
                ctx.mm(b, b.t[:, 0:128],
                       fm(A1, c, tc * 128, tc * 128 + 128),
                       s0.tile(c),
                       first=(c == 0), last=(c == NCH - 1), slots=[s0],
                       waits=ctx.zw(tc // 4))
            ctx.ev("vector", "tensor_copy", banks=[b],
                   out=tmv(A3, tc, jb * 128, jb * 128 + 128),
                   in_=b.t[:, 0:128])

    qp = A4[:, 0:2048]
    kp = A4[:, 2048:4096]
    Pscr = [A4[:, 4096 + i * 512: 4096 + (i + 1) * 512] for i in range(4)]
    rbrow = (A4[0:1, 6144:6656], A4[64:65, 6144:6656])

    ctx.load_slice(P13[0], w["wqk"], 0, 256)
    for hp in range(NH // 2):
        s13 = P13[hp % 2]
        if hp + 1 < NH // 2:
            ctx.load_slice(P13[(hp + 1) % 2], w["wqk"], (hp + 1) * 256, 256)
        for tau in range(NTAU):
            t0 = tau * 512
            for c in range(NCH):
                ctx.mm(pA, pA.t[:, 0:512], s13.tile(c, 0),
                       fm(A1, c, t0, t0 + 512),
                       first=(c == 0), last=(c == NCH - 1), slots=[s13],
                       waits=ctx.zw(tau))
            ctx.ev("vector", "tensor_copy", banks=[pA],
                   out=qp[:, t0:t0 + 512], in_=pA.t[:, 0:512])
            for c in range(NCH):
                ctx.mm(pB, pB.t[:, 0:512], s13.tile(c, 128),
                       fm(A1, c, t0, t0 + 512),
                       first=(c == 0), last=(c == NCH - 1), slots=[s13],
                       waits=ctx.zw(tau))
            ctx.ev("vector", "tensor_copy", banks=[pB],
                   out=kp[:, t0:t0 + 512], in_=pB.t[:, 0:512])
        qkv = p.cnt["dve"]
        pmark = [0, 0, 0, 0]

        def norm_chain(nqt):
            rbv = [0, 0]
            for h01 in range(2):
                b0 = h01 * 64
                bz = pS1 if h01 == 0 else pS2
                zrow = R1[b0:b0 + 1, 0:512]
                lrow = R1[b0:b0 + 1, 512:1024]
                ctx.ev("vector", "tensor_copy", banks=[bz],
                       out=zrow, in_=bz.t[b0:b0 + 1, 0:512])
                dv = p.cnt["dve"]
                ctx.ev("scalar", "activation", waits=[("dve", dv)],
                       out=lrow, in_=zrow, func=AF.Ln)
                av = ctx.ev("scalar", "activation", out=zrow, in_=lrow,
                            func=AF.Exp, scale=-1.0)
                rbv[h01] = ctx.ev("vector", "tensor_copy",
                                  waits=[("act", av)],
                                  out=rbrow[h01], in_=zrow)
            return rbv

        def norm_apply(nqt, rbv):
            t0 = nqt * 512
            for h01 in range(2):
                b0 = h01 * 64
                bb = pTf if h01 == 0 else pD
                ctx.mm(bb, bb.t[b0:b0 + 64, 0:512],
                       t["onesrX"][b0:b0 + 1, 0:64], rbrow[h01],
                       first=True, last=True,
                       tp=(0, b0), waits=[("dve", rbv[h01])])
            for h01 in range(2):
                b0 = h01 * 64
                bb = pTf if h01 == 0 else pD
                seg = A2[b0:b0 + 64, hp * T + t0: hp * T + t0 + 512]
                ctx.ev("vector", "tensor_tensor", banks=[bb],
                       out=seg, in0=seg, in1=bb.t[b0:b0 + 64, 0:512],
                       op=ALU.mult)


        pend = None
        for qt in range(NTAU):
            t0 = qt * 512
            ntile = 4 * qt + 4
            def st(h01, Ti):
                b = pA if h01 == 0 else pB
                j = Ti - 4 * qt
                r0 = 128 * j if j > 0 else 0
                lhs = kp[h01 * 64:(h01 + 1) * 64, Ti * 128:(Ti + 1) * 128]
                rhs = qp[h01 * 64:(h01 + 1) * 64, t0 + r0:t0 + 512]
                ctx.mm(b, b.t[:, r0:512], lhs, rhs,
                       first=True, last=(j < 0), waits=[("dve", qkv)])
                if j >= 0:
                    ctx.mm(b, b.t[:, r0:r0 + 128], ctx.c_ident,
                           t["mbig"][:, 0:128],
                           first=False, last=True)

            st(0, 0)
            st(1, 0)
            if pend is not None:
                norm_apply(qt - 1, pend)
                pend = None
            for Ti in range(ntile):
                last_t = (Ti == ntile - 1)
                for h01 in range(2):
                    b = pA if h01 == 0 else pB
                    bc = pC if h01 == 0 else pD
                    bz = pS1 if h01 == 0 else pS2
                    b0 = h01 * 64
                    h = hp * 2 + h01
                    Pz = Pscr[(Ti % 2) * 2 + h01]
                    j = Ti - 4 * qt
                    r0 = 128 * j if j > 0 else 0
                    evx = ctx.ev("scalar", "activation", banks=[b],
                                 waits=[("pe", pmark[(Ti % 2) * 2 + h01])],
                                 out=Pz[:, r0:512], in_=b.t[:, r0:512],
                                 func=AF.Exp, scale=0.125)
                    vsl = tmv(A3, Ti, h * 64, h * 64 + 64)
                    ctx.mm(bc, bc.t[b0:b0 + 64, r0:512],
                           vsl, Pz[:, r0:512], first=(Ti == 0), last=last_t,
                           pinc=False, tp=(0, b0), waits=[("act", evx)])
                    ctx.mm(bz, bz.t[b0:b0 + 1, r0:512],
                           t["onesb"][:, 0:1], Pz[:, r0:512],
                           first=(Ti == 0), last=last_t,
                           pinc=True, tp=(0, b0))
                    pmark[(Ti % 2) * 2 + h01] = p.cnt["pe"]
                    if not last_t:
                        st(h01, Ti + 1)

            for bb in (pC, pD, pS1, pS2):
                bb.ready = p.cnt["pe"]
                bb.started = False
            pend = norm_chain(qt)
            ctx.ev("vector", "tensor_copy", banks=[pC],
                   out=A2[0:64, hp * T + t0: hp * T + t0 + 512],
                   in_=pC.t[0:64, 0:512])
            ctx.ev("vector", "tensor_copy", banks=[pD],
                   out=A2[64:128, hp * T + t0: hp * T + t0 + 512],
                   in_=pD.t[64:128, 0:512])

        norm_apply(NTAU - 1, pend)
    ctx.barrier()
    for jc in range(NCH):
        s0 = S[2 + jc % 2]
        if jc + 1 < NCH:
            ctx.load_slice(S[2 + (jc + 1) % 2], w["wo"], (jc + 1) * 128, 128)
        for tau in range(NTAU):
            t0 = tau * 512
            b = pA if tau % 2 == 0 else pB
            for c in range(NCH):
                ctx.mm(b, b.t[:, 0:512], s0.tile(c),
                       fm(A2, c, t0, t0 + 512),
                       first=(c == 0), last=(c == NCH - 1), slots=[s0])
            tscr = (t["SCRF"][:, 0:512] if tau % 2 == 0
                    else t["SCRF"][:, 512:1024])
            av = ctx.ev("scalar", "activation", banks=[b],
                        waits=[("dve", p.cnt["dve"])],
                        out=tscr, in_=b.t[:, 0:512], func=AF.Identity,
                        bias=t["boc"][:, jc:jc + 1])
            ctx.ev("vector", "tensor_tensor", waits=[("act", av)],
                   out=fm(xs, jc, t0, t0 + 512),
                   in0=fm(xs, jc, t0, t0 + 512), in1=tscr, op=ALU.add)


def emit_attn0(ctx, w):
    p, t = ctx.p, ctx.t
    xs = t["xs"]
    A1, A2, A3, A4 = t["A1"], t["A2"], t["A3"], t["A4"]
    pA, pB, pC, pD = (ctx.banks[k] for k in ("pA", "pB", "pC", "pD"))
    pS1 = ctx.banks["pS1"]
    pT = ctx.banks["pT"]
    pTf = ctx.banks["pTf"]
    S = t["slotS"]
    qkvT = w["qkvT"]

    ctx.load_slice(S[0], qkvT, D, 128)
    phases = int(os.environ.get("ATTN0_PHASES", "9"))
    scanp = int(os.environ.get("ATTN0_SCAN", "9"))
    emit_ln(ctx, A1)
    if phases < 1:
        return

    kmx = t["kmx"]
    k4 = t["k4"]
    for kc in range(NCH):
        s0 = S[kc % 2]
        if kc + 1 < NCH:
            ctx.load_slice(S[(kc + 1) % 2], qkvT, D + (kc + 1) * 128, 128)
        else:
            ctx.load_slice(S[2], qkvT, 0, 128)
        for tau in range(NTAU):
            b = pA if tau % 2 == 0 else pB
            for c in range(NCH):
                ctx.mm(b, b.t[:, 0:512], s0.tile(c),
                       fm(A1, c, tau * 512, tau * 512 + 512),
                       first=(c == 0), last=(c == NCH - 1), slots=[s0],
                       waits=ctx.zw(tau))
            ctx.ev("vector", "reduce_max", banks=[b],
                   out=k4[:, tau:tau + 1], in_=b.t[:, 0:512], axis=AX.X)
        ctx.ev("vector", "reduce_max", out=kmx[:, kc:kc + 1], in_=k4,
               axis=AX.X, negate=True)
    kmxv = p.cnt["dve"]

    S_f = t["SCRF"][:, 512:1024]
    ctx.ev("vector", "memset", ap=S_f, constant=0.0)
    sprev = t["R1"][0:1, 0:D]
    ctx.ev("vector", "memset", ap=sprev, constant=1e-9)

    for hf in range(2):
        hb = hf * 1024
        other = 1 - hf
        sb_ = A2[:, other * 8192: other * 8192 + 8192]
        ketm = sb_[:, 0:1024]
        qh = sb_[:, 1024:2048]
        qhf = sb_[:, 2048:3072]
        am = sb_[:, 3072:3328]
        S_b = sb_[:, 3328:3840]
        if hf == 0:
            ctx.ev("vector", "memset", ap=S_b, constant=0.0)

        if phases < 2:
            continue
        qev = 0
        for jb in range(NCH):
            s0 = S[2 + jb % 2]
            if jb + 1 < NCH:
                ctx.load_slice(S[2 + (jb + 1) % 2], qkvT,
                               (jb + 1) * 128, 128)
            else:
                ctx.load_slice(S[2], qkvT, 2 * D, 128)
            for tc2 in range(NTC // 2):
                tc = hf * 8 + tc2
                b = pA if tc2 % 2 == 0 else pB
                for c in range(NCH):
                    ctx.mm(b, b.t[:, 0:128],
                           fm(A1, c, tc * 128, tc * 128 + 128),
                           s0.tile(c),
                           first=(c == 0), last=(c == NCH - 1), slots=[s0],
                           waits=ctx.zw(tc // 4))
                qe = t["qe"]
                av = ctx.ev("scalar", "activation", banks=[b],
                            waits=[("dve", qev)],
                            out=qe, in_=b.t[:, 0:128], func=AF.Exp)
                ctx.ev("vector", "reduce_sum", waits=[("act", av)],
                       out=t["sc8a"][:, 0:2],
                       in_=qe.rearrange("p (h c) -> p h c", c=HS),
                       axis=AX.X)
                ctx.ev("vector", "reciprocal", out=t["sc8b"][:, 0:2],
                       in_=t["sc8a"][:, 0:2])
                ctx.ev("vector", "tensor_scalar", out=t["sc8b"][:, 0:2],
                       in0=t["sc8b"][:, 0:2], scalar1=0.125, scalar2=None,
                       op0=ALU.mult)
                for h2 in range(2):
                    qev = ctx.ev("vector", "tensor_scalar",
                                 out=tmv(A2, tc, jb * 128 + h2 * 64,
                                         jb * 128 + h2 * 64 + 64),
                                 in0=qe[:, h2 * 64:(h2 + 1) * 64],
                                 scalar1=t["sc8b"][:, h2:h2 + 1],
                                 scalar2=None, op0=ALU.mult)

        if phases < 3:
            continue
        for kc in range(NCH):
            s0 = S[kc % 2]
            if kc == 0:
                ctx.load_slice(s0, qkvT, D, 128)
            if kc + 1 < NCH:
                ctx.load_slice(S[(kc + 1) % 2], qkvT, D + (kc + 1) * 128, 128)
            for tau2 in range(2):
                b = pA if tau2 % 2 == 0 else pB
                tb = hb + tau2 * 512
                for c in range(NCH):
                    ctx.mm(b, b.t[:, 0:512], s0.tile(c),
                           fm(A1, c, tb, tb + 512),
                           first=(c == 0), last=(c == NCH - 1), slots=[s0],
                           waits=ctx.zw(tb // 512))
                ctx.ev("scalar", "activation", banks=[b],
                       waits=[("dve", kmxv)],
                       out=fm(A3, kc, tb, tb + 512), in_=b.t[:, 0:512],
                       func=AF.Exp, bias=kmx[:, kc:kc + 1])

        if phases < 4:
            continue
        for jb in range(NCH):
            s0 = S[2 + jb % 2]
            if jb + 1 < NCH:
                ctx.load_slice(S[2 + (jb + 1) % 2], qkvT,
                               2 * D + (jb + 1) * 128, 128)
            for tc2 in range(NTC // 2):
                tc = hf * 8 + tc2
                b = pA if tc2 % 2 == 0 else pB
                for c in range(NCH):
                    ctx.mm(b, b.t[:, 0:128],
                           fm(A1, c, tc * 128, tc * 128 + 128),
                           s0.tile(c),
                           first=(c == 0), last=(c == NCH - 1), slots=[s0],
                           waits=ctx.zw(tc // 4))
                ctx.ev("vector", "tensor_copy", banks=[b],
                       out=A4[:, tc2 * D + jb * 128: tc2 * D + jb * 128 + 128],
                       in_=b.t[:, 0:128])
        ctx.barrier()

        if phases < 5:
            continue
        kexp_act = p.cnt["act"]
        qsm_dve = p.cnt["dve"]
        qtrans_prev = 0
        for ck2 in range(8):
            ck = hf * 8 + ck2
            cb = ck * 128
            for c in range(NCH):
                ctx.trans(pT, pT.t[:, c * 128:(c + 1) * 128],
                          fm(A3, c, cb, cb + 128),
                          first=(c == 0), last=(c == NCH - 1),
                          waits=[("act", kexp_act)])
            ctx.ev("vector", "tensor_copy", banks=[pT], out=ketm,
                   in_=pT.t[:, 0:1024])
            kv = p.cnt["dve"]
            if scanp < 1:
                continue
            for half in range(2):
                b = pA if half == 0 else pB
                ctx.mm(b, b.t[:, 0:512], t["lte2"][:, 0:128],
                       ketm[:, half * 512:(half + 1) * 512],
                       first=True, last=False, waits=[("dve", kv)])
                ctx.mm(b, b.t[:, 0:512], t["onesrf"][0:1, :],
                       sprev[0:1, half * 512:(half + 1) * 512],
                       first=False, last=True)
            for half in range(2):
                b = pA if half == 0 else pB
                lnz = t["R1"][:, 1024:1536]
                rz = t["R1"][:, 1536:2048]
                ctx.ev("scalar", "activation", banks=[b],
                       waits=[("dve", p.cnt["dve"])], out=lnz,
                       in_=b.t[:, 0:512], func=AF.Ln)
                av = ctx.ev("scalar", "activation", out=rz, in_=lnz,
                            func=AF.Exp, scale=-1.0)
                ctx.ev("vector", "tensor_tensor",
                       waits=[("act", av), ("pe", qtrans_prev)],
                       out=qh[:, half * 512:(half + 1) * 512],
                       in0=tmv(A2, ck, half * 512, half * 512 + 512),
                       in1=rz, op=ALU.mult)
            qv = p.cnt["dve"]
            for c in range(NCH):
                ctx.trans(pT, pT.t[:, c * 128:(c + 1) * 128],
                          qh[:, c * 128:(c + 1) * 128],
                          first=(c == 0), last=(c == NCH - 1),
                          waits=[("dve", qv)])
            ctx.ev("vector", "tensor_copy", banks=[pT], out=qhf,
                   in_=pT.t[:, 0:1024])
            qfv = p.cnt["dve"]
            qtrans_prev = pT.ready
            pS2 = ctx.banks["pS2"]
            for half in range(2):
                bcs = pS1 if half == 0 else pS2
                ctx.mm(bcs, bcs.t[0:1, 0:512], t["onesb"][:, 0:1],
                       ketm[:, half * 512:(half + 1) * 512],
                       first=True, last=True, waits=[("dve", kv)])
            svmark = p.cnt["dve"]
            svmark_a = p.cnt["act"]
            if scanp < 3:
                continue
            for hp2 in range(NH // 2):
                he, ho = hp2 * 2, hp2 * 2 + 1
                for h01, h in ((0, he), (1, ho)):
                    ba = pA if h01 == 0 else pB
                    lhs = A3[(h % 2) * 64:(h % 2) * 64 + 64,
                             (h // 2) * T + cb:(h // 2) * T + cb + 128]
                    rhs = qhf[(h % 2) * 64:(h % 2) * 64 + 64,
                              (h // 2) * 128:(h // 2) * 128 + 128]
                    ctx.mm(ba, ba.t[:, 0:128], lhs, rhs,
                           first=True, last=True,
                           waits=[("dve", qfv)])
                ctx.ev("vector", "tensor_tensor", banks=[pA],
                       out=am[:, 0:128], in0=pA.t[:, 0:128],
                       in1=t["lte2"][:, 0:128], op=ALU.mult)
                mv = ctx.ev("vector", "tensor_tensor", banks=[pB],
                            out=am[:, 128:256], in0=pB.t[:, 0:128],
                            in1=t["lte2"][:, 128:256], op=ALU.mult)
                if scanp < 4:
                    continue
                for h01, h in ((0, he), (1, ho)):
                    b0 = h01 * 64
                    ctx.mm(pD, pD.t[b0:b0 + 64, 0:128],
                           S_b[b0:b0 + 64, hp2 * 64:(hp2 + 1) * 64],
                           qhf[b0:b0 + 64,
                               (h // 2) * 128:(h // 2) * 128 + 128],
                           first=True, last=False, pinc=False, tp=(0, b0),
                           waits=[("dve", svmark), ("act", svmark_a)])
                    ctx.mm(pD, pD.t[b0:b0 + 64, 0:128],
                           A4[:, ck2 * D + h * 64: ck2 * D + (h + 1) * 64],
                           am[:, h01 * 128:(h01 + 1) * 128],
                           first=False, last=True, pinc=(h01 == 1),
                           tp=(0, b0), waits=[("dve", mv)])
                for h01, h in ((0, he), (1, ho)):
                    b0 = h01 * 64
                    ctx.mm(pTf, pTf.t[b0:b0 + 64, 0:64],
                           ketm[:, h * 64:(h + 1) * 64],
                           A4[:, ck2 * D + h * 64: ck2 * D + (h + 1) * 64],
                           first=True, last=True, pinc=(h01 == 1),
                           tp=(0, b0))
                ctx.ev("scalar", "copy", banks=[pD],
                       out=A1[:, hp2 * T + cb: hp2 * T + cb + 128],
                       in_=pD.t[:, 0:128])
                if scanp < 5:
                    continue
                ctx.ev("vector", "tensor_tensor", banks=[pTf],
                       out=S_f[:, hp2 * 64:(hp2 + 1) * 64],
                       in0=S_f[:, hp2 * 64:(hp2 + 1) * 64],
                       in1=pTf.t[:, 0:64], op=ALU.add)
                ctx.ev("scalar", "copy", waits=[("dve", p.cnt["dve"])],
                       out=S_b[:, hp2 * 64:(hp2 + 1) * 64],
                       in_=S_f[:, hp2 * 64:(hp2 + 1) * 64])
            for half in range(2):
                bcs = pS1 if half == 0 else pS2
                ctx.ev("vector", "tensor_tensor", banks=[bcs],
                       out=sprev[0:1, half * 512:(half + 1) * 512],
                       in0=sprev[0:1, half * 512:(half + 1) * 512],
                       in1=bcs.t[0:1, 0:512], op=ALU.add)
        ctx.barrier()

    if phases < 6:
        return
    ctx.load_slice(S[0], w["outT"], 0, 128)
    for jc in range(NCH):
        s0 = S[jc % 2]
        if jc + 1 < NCH:
            ctx.load_slice(S[(jc + 1) % 2], w["outT"], (jc + 1) * 128, 128)
        for tau in range(NTAU):
            t0 = tau * 512
            b = (pA, pB, pC, pD)[tau]
            for c in range(NCH):
                ctx.mm(b, b.t[:, 0:512], s0.tile(c),
                       fm(A1, c, t0, t0 + 512),
                       first=(c == 0), last=(c == NCH - 1), slots=[s0])
            ctx.ev("vector", "tensor_tensor", banks=[b],
                   out=fm(xs, jc, t0, t0 + 512),
                   in0=fm(xs, jc, t0, t0 + 512),
                   in1=b.t[:, 0:512], op=ALU.add)


class BlockShim:
    def __init__(self, blk):
        self.blk = blk

    def __getattr__(self, name):
        dec = getattr(self.blk, name)

        def call(fn):
            dec(fn)
        return call


def build_program(subids):
    nc = bass.Bass("TRN2", target_bir_lowering=False)
    h = {}
    h["xin"] = nc.dram_tensor("xin", [NSEQ, D, T], F32,
                              kind="ExternalInput").ap()
    h["xout"] = nc.dram_tensor("xout", [NSEQ, D, T], F32,
                               kind="ExternalOutput").ap()

    def din(name, shape, dt=F32):
        h[name] = nc.dram_tensor(name, shape, dt, kind="ExternalInput").ap()

    needs = set(subids)
    if needs & {FFN0A, FFN0B}:
        din("w13", [NE, D, 2 * D]); din("w2", [NE, D, D])
        din("sw13", [D, 2 * D]); din("sw2", [D, D])
        din("rw", [D, NE]); din("ebbc", [128, NE])
    if ATTN0 in needs:
        din("wqkvT", [D, 3 * D]); din("woutT", [D, D])
    if ATTN1 in needs:
        din("wqk", [D, 2 * D]); din("wv", [D, D]); din("wo", [D, D])
        din("bo", [D])
    if needs & {FFN1A, FFN1B}:
        din("cbt", [D, D]); din("scb", [D]); din("l1b", [D])
    din("c_ident", [128, 128], BF16)
    din("c_lte2", [128, 256], BF16)
    din("c_mbig", [128, 128], BF16)

    prog = Prog()
    with ExitStack() as st:
        e = st.enter_context
        t = {}

        def sb(name, shape, dt):
            t[name] = e(nc.sbuf_tensor(name, shape, dt))[
                tuple(slice(None) for _ in shape)]

        sb("xs", [128, NCH * T], F32)
        sb("A1", [128, NCH * T], BF16)
        sb("A2", [128, NCH * T], BF16)
        sb("A3", [128, NCH * T], BF16)
        sb("A4", [128, 8192], BF16)
        sb("wsl", [128, 8192], BF16)
        sb("R1", [128, T], F32)
        sb("SCRF", [128, 1024], F32)
        sb("rwt", [128, NCH * NE], BF16)
        sb("ebbct", [128, NE], F32)
        sb("kmx", [128, NCH], F32)
        sb("k4", [128, 4], F32)
        for nm in ("sc8a", "sc8b", "sc8c", "sc8x", "sc8y", "sc8z"):
            sb(nm, [128, NE], F32)
        for nm in ("sc1a", "sc1b", "epsc"):
            sb(nm, [128, 1], F32)
        sb("boc", [128, NCH], F32)
        sb("scbh", [128, NCH], F32)
        sb("l1bh", [128, NCH], F32)
        sb("onesb", [128, 1], BF16)
        sb("onesrX", [128, 128], BF16)
        sb("onesrf", [1, 128], F32)
        sb("identb", [128, 128], BF16)
        sb("lte2", [128, 256], BF16)
        sb("mbig", [128, 128], BF16)

        t["gts"] = t["A4"][:, 0:NTC * NE]
        t["qe"] = t["A4"][:, 256:384]
        t["xsc"] = t["A4"][:, 2048:4096]

        banks = {}
        pab = e(nc.psum_tensor("pAB", [128, 1024], F32))[:, :]
        t["pAB"] = pab
        banks["pA"] = Bank(pab[:, 0:512])
        banks["pB"] = Bank(pab[:, 512:1024])
        for nm in ("pC", "pD", "pS1", "pS2"):
            banks[nm] = Bank(e(nc.psum_tensor(nm, [128, 512], F32))[:, :])
        banks["pT"] = Bank(e(nc.psum_tensor("pT", [128, 1024], BF16))[:, :])
        banks["pTf"] = Bank(e(nc.psum_tensor("pTf", [128, 512], F32))[:, :])

        sems = {nm: e(nc.semaphore("s_" + nm))
                for nm in ("pe", "dve", "act", "g", "h",
                           "g0", "g1", "g2", "g3", "g4", "g5")}

        ctx = Ctx(nc, prog)
        ctx.t = t
        ctx.banks = banks
        ctx.c_ident = t["identb"]
        t["slotP"] = [Slot(t["wsl"][:, 0:2048], 256, "g0"),
                      Slot(t["wsl"][:, 2048:4096], 256, "g1")]
        t["slotS"] = [Slot(t["wsl"][:, 4096 + i * 1024:
                                    4096 + (i + 1) * 1024], 128,
                           "g%d" % (2 + i))
                      for i in range(4)]

        p = prog
        p.dma("sync", t["identb"], h["c_ident"][:, :], "h")
        p.dma("sync", t["lte2"], h["c_lte2"][:, :], "h")
        p.dma("sync", t["mbig"], h["c_mbig"][:, :], "h")
        if needs & {FFN0A, FFN0B}:
            p.dma("gpsimd", t["rwt"].rearrange("p (c e) -> p c e", c=NCH),
                  h["rw"].rearrange("(c p) e -> p c e", p=128), "g")
            p.dma("sync", t["ebbct"], h["ebbc"][:, :], "h")
        if ATTN1 in needs:
            bor = h["bo"].rearrange("(c p one) -> c p one", p=128, one=1)
            for c in range(NCH):
                p.dma("sync", t["boc"][:, c:c + 1], bor[c], "h")
        if needs & {FFN1A, FFN1B}:
            scbr = h["scb"].rearrange("(c p one) -> c p one", p=128, one=1)
            l1br = h["l1b"].rearrange("(c p one) -> c p one", p=128, one=1)
            for c in range(NCH):
                p.dma("sync", t["scbh"][:, c:c + 1], scbr[c], "h")
                p.dma("sync", t["l1bh"][:, c:c + 1], l1br[c], "h")
        hload = p.cnt["h"]
        gload = p.cnt["g"]
        for eng in ("vector", "scalar", "tensor"):
            p.wait(eng, "h", hload)
            p.wait(eng, "g", gload)
        prog.op("vector", "memset", ap=t["onesb"], constant=1.0)
        prog.op("vector", "memset", ap=t["epsc"], constant=1e-5)
        prog.op("vector", "memset", ap=t["onesrX"], constant=1.0)
        if needs & {FFN1A, FFN1B}:
            prog.op("vector", "tensor_scalar", out=t["scbh"], in0=t["scbh"],
                    scalar1=1.0 / 254.0, scalar2=None, op0=ALU.mult)
            prog.op("vector", "tensor_scalar", out=t["l1bh"], in0=t["l1bh"],
                    scalar1=0.5, scalar2=None, op0=ALU.mult)
        mv = prog.op("vector", "memset", ap=t["onesrf"], constant=1.0,
                     inc=("dve", 1))
        for eng in ("scalar", "tensor"):
            p.wait(eng, "dve", mv)
        p.wait("gpsimd", "dve", mv)

        for seq in range(NSEQ):
            xr = h["xin"].rearrange("s (c p) t -> s c p t", p=128)
            p.wait("sync", "dve", p.cnt["dve"])
            p.wait("sync", "pe", p.cnt["pe"])
            p.wait("sync", "h", p.cnt["h"])
            for c in range(NCH):
                p.dma("sync", fm(t["xs"], c, 0, T), xr[seq, c], "h")
            hx = p.cnt["h"]
            for eng in ("vector", "scalar", "tensor"):
                p.wait(eng, "h", hx)
            for sid in subids:
                kind = EMITTERS[sid]
                if kind == "ffn0":
                    emit_moe_ffn(ctx, {
                        "w13": [h["w13"][ei] for ei in range(NE)],
                        "w2": [h["w2"][ei] for ei in range(NE)],
                        "sw13": h["sw13"], "sw2": h["sw2"]})
                elif kind == "ffn1":
                    emit_l1ffn(ctx, {"cbt": h["cbt"]})
                elif kind == "attn0":
                    emit_attn0(ctx, {"qkvT": h["wqkvT"], "outT": h["woutT"]})
                elif kind == "attn1":
                    emit_attn1(ctx, {"wqk": h["wqk"], "wv": h["wv"],
                                     "wo": h["wo"]})
                ctx.barrier()
            xo = h["xout"].rearrange("s (c p) t -> s c p t", p=128)
            p.wait("sync", "dve", p.cnt["dve"])
            p.wait("sync", "pe", p.cnt["pe"])
            for c in range(NCH):
                p.dma("sync", xo[seq, c], fm(t["xs"], c, 0, T), "h")
        p.wait("sync", "h", p.cnt["h"])

        blk = e(nc.Block())
        prog.replay(BlockShim(blk), sems)
    return nc


def _consts():
    ident = np.eye(128, dtype=np.float32).astype(BFNP)
    lte = (np.arange(128)[:, None] <= np.arange(128)[None, :])
    lte2 = np.concatenate([lte, lte], axis=1).astype(np.float32).astype(BFNP)
    s = np.arange(128)[:, None]
    u = np.arange(128)[None, :]
    mbig = np.where(s <= u, 0.0, -1e9).astype(np.float32).astype(BFNP)
    return ident, lte2, mbig


def _interleave128(a, b):
    R, N = a.shape
    return np.concatenate([a.reshape(R, N // 128, 128),
                           b.reshape(R, N // 128, 128)],
                          axis=2).reshape(R, 2 * N)


def host_inputs(inputs):
    ident, lte2, mbig = _consts()

    def f32(a):
        return np.ascontiguousarray(np.asarray(a, dtype=np.float32))

    w1 = np.asarray(inputs["L0_ffn_w1"], dtype=np.float32)
    w3 = np.asarray(inputs["L0_ffn_w3"], dtype=np.float32)
    w13 = np.ascontiguousarray(
        np.stack([_interleave128(w1[ei], w3[ei]) for ei in range(NE)]))
    sw13 = np.ascontiguousarray(_interleave128(
        np.asarray(inputs["L0_ffn_sw1"], dtype=np.float32),
        np.asarray(inputs["L0_ffn_sw3"], dtype=np.float32)))
    wqk = np.ascontiguousarray(_interleave128(
        np.asarray(inputs["L1_attn_Wq"], dtype=np.float32),
        np.asarray(inputs["L1_attn_Wk"], dtype=np.float32)))
    common = {
        "w13": w13,
        "w2": f32(inputs["L0_ffn_w2"]),
        "sw13": sw13,
        "sw2": f32(inputs["L0_ffn_sw2"]),
        "rw": f32(inputs["L0_ffn_router_w"]),
        "ebbc": np.ascontiguousarray(np.tile(
            np.asarray(inputs["L0_ffn_expert_bias"],
                       dtype=np.float32)[None, :], (128, 1))),
        "wqkvT": f32(np.asarray(inputs["L0_attn_W_qkv"]).T),
        "woutT": f32(np.asarray(inputs["L0_attn_W_out"]).T),
        "wqk": wqk,
        "wv": f32(inputs["L1_attn_Wv"]),
        "wo": f32(inputs["L1_attn_Wo"]),
        "bo": f32(inputs["L1_attn_bo"]),
        "cbt": f32(np.asarray(inputs["L1_ffn_CB"]).T),
        "scb": f32(inputs["L1_ffn_SCB"]),
        "l1b": f32(inputs["L1_ffn_bias"]),
        "c_ident": ident, "c_lte2": lte2, "c_mbig": mbig,
    }
    return common


USED = {
    "ffn0": ["w13", "w2", "sw13", "sw2", "rw", "ebbc"],
    "attn0": ["wqkvT", "woutT"],
    "attn1": ["wqk", "wv", "wo", "bo"],
    "ffn1": ["cbt", "scb", "l1b"],
}


def kernel(**inputs):
    x = np.asarray(inputs["L0_attn_x"], dtype=np.float32)
    B = x.shape[0]
    percore = B // 8
    xcur = np.ascontiguousarray(
        x.reshape(8, percore, T, D).transpose(0, 1, 3, 2))
    common = host_inputs(inputs)
    for group in LAUNCH_GROUPS:
        names = {"c_ident", "c_lte2", "c_mbig"}
        for sid in group:
            names.update(USED[EMITTERS[sid]])
        nc = build_program(group)
        in_maps = [dict({n: common[n] for n in names}, xin=xcur[c])
                   for c in range(8)]
        res = run_bass_kernel_spmd(nc, in_maps, core_ids=list(range(8)))
        xcur = np.stack([res.results[c]["xout"] for c in range(8)])
    out = np.ascontiguousarray(
        xcur.transpose(0, 1, 3, 2).reshape(B, T, D)).astype(np.float32)
    return out


# version 14
# speedup vs baseline: 1.1123x; 1.1123x over previous
import os
import numpy as np
import ml_dtypes
from contextlib import ExitStack

import concourse.bass as bass
import concourse.mybir as mybir
from concourse.bass_utils import run_bass_kernel_spmd
from concourse.bass_types import AP

F32 = mybir.dt.float32
BF16 = mybir.dt.bfloat16
AF = mybir.ActivationFunctionType
ALU = mybir.AluOpType
AX = mybir.AxisListType
BFNP = ml_dtypes.bfloat16

D = 1024
T = 2048
NCH = 8
NTC = 16
NTAU = 4
NH = 16
HS = 64
NE = 8
NSEQ = 2

FFN0A, ATTN0, FFN0B, FFN1A, ATTN1, FFN1B = range(6)
ALL_SUBS = [FFN0A, ATTN0, FFN0B, FFN1A, ATTN1, FFN1B]
EMITTERS = {FFN0A: "ffn0", ATTN0: "attn0", FFN0B: "ffn0",
            FFN1A: "ffn1", ATTN1: "attn1", FFN1B: "ffn1"}

LAUNCH_GROUPS = [ALL_SUBS]

ENGSEM = {"vector": "dve", "scalar": "act", "tensor": "pe", "gpsimd": "pool"}


def _regions(kwargs):
    reads, writes = [], []
    for k, v in kwargs.items():
        if not isinstance(v, AP):
            continue
        try:
            pairs = list(v.ap)
            w = pairs[0][0]
            off = v.offset
            p0 = off // w if w else 0
            c0 = off % w if w else off
            pcnt = pairs[0][1]
            span = 1
            for stp, cn in pairs[1:]:
                span += (cn - 1) * abs(stp)
            reg = (v.tensor.name, p0, p0 + pcnt, c0, c0 + span)
        except Exception:
            reg = (v.tensor.name, 0, 1 << 30, 0, 1 << 30)
        if k in ("out", "accum_out", "ap"):
            writes.append(reg)
        else:
            reads.append(reg)
    return reads, writes


def _overlap(a, b):
    return (a[0] == b[0] and a[1] < b[2] and b[1] < a[2]
            and a[3] < b[4] and b[3] < a[4])


class Prog:
    ENGS = ("sync", "gpsimd", "vector", "scalar", "tensor")

    def __init__(self):
        self.steps = {e: [] for e in self.ENGS}
        self.cnt = {s: 0 for s in
                    ("pe", "dve", "act", "pool", "g", "h",
                     "g0", "g1", "g2", "g3", "g4", "g5")}
        self.maxwait = {}
        self.dirty = {e: [] for e in self.ENGS}
        self.rdirty = {e: [] for e in self.ENGS}

    def wait(self, eng, sem, val):
        if val <= 0:
            return
        if self.maxwait.get((eng, sem), -1) >= val:
            return
        self.maxwait[(eng, sem)] = val
        self.steps[eng].append(("wait", sem, val))

    def op(self, eng, method, inc=None, **kwargs):
        if eng in ("vector", "scalar"):
            reads, writes = _regions(kwargs)
            need = any(_overlap(r, d) for r in reads for d in self.dirty[eng])
            if not need:
                need = any(_overlap(wr, d) for wr in writes
                           for d in self.dirty[eng] + self.rdirty[eng])
            if need:
                self.steps[eng].append(("drain",))
                self.dirty[eng].clear()
                self.rdirty[eng].clear()
            self.dirty[eng].extend(writes)
            self.rdirty[eng].extend(reads)
        incs = []
        if inc is not None:
            sem, amt = inc if isinstance(inc, tuple) else (inc, 1)
            self.cnt[sem] += amt
            incs.append((sem, amt))
        self.steps[eng].append(("op", method, kwargs, incs))
        return self.cnt[incs[0][0]] if incs else None

    def dma(self, eng, out, in_, sem):
        self.cnt[sem] += 16
        self.steps[eng].append(("op", "dma_start", {"out": out, "in_": in_},
                                [(sem, 16)]))
        return self.cnt[sem]

    def replay(self, block, sems):
        for eng_name in self.ENGS:
            steps = self.steps[eng_name]
            if not steps:
                continue

            def body(eng, _steps=steps):
                for st in _steps:
                    if st[0] == "wait":
                        eng.wait_ge(sems[st[1]], st[2])
                    elif st[0] == "drain":
                        eng.drain()
                    else:
                        _, method, kwargs, incs = st
                        try:
                            ins = getattr(eng, method)(**kwargs)
                        except Exception:
                            import sys
                            print("REPLAY FAIL:", eng_name, method,
                                  {k: (str(v.tensor.name) + str(list(v.ap)) +
                                       f"@{v.offset}")
                                   if isinstance(v, AP) else v
                                   for k, v in kwargs.items()},
                                  file=sys.stderr)
                            raise
                        for sem, amt in incs:
                            ins.then_inc(sems[sem], amt)
            getattr(block, eng_name)(body)


class Bank:
    def __init__(self, tile):
        self.t = tile
        self.ready = 0
        self.free = {}
        self.started = False


class Slot:
    def __init__(self, ap, pitch, sem):
        self.ap = ap
        self.pitch = pitch
        self.sem = sem
        self.ready = 0
        self.mark = 0

    def tile(self, c, off=0, wdt=128):
        return self.ap[:, c * self.pitch + off: c * self.pitch + off + wdt]


class Ctx:
    def __init__(self, nc, prog):
        self.nc = nc
        self.p = prog
        self.t = {}
        self.banks = {}
        self.c_ident = None
        self.zv = [0] * NTAU

    def mm(self, bank, out, lhsT, rhs, first, last, pinc=None, tp=None,
           slots=(), waits=()):
        p = self.p
        if first and not bank.started:
            for eng, v in bank.free.items():
                p.wait("tensor", ENGSEM[eng], v)
            bank.free = {}
            bank.started = True
        for s in slots:
            p.wait("tensor", s.sem, s.ready)
        for sem, v in waits:
            p.wait("tensor", sem, v)
        if pinc is None:
            pinc = last
        kw = dict(out=out, lhsT=lhsT, rhs=rhs, start=first, stop=last)
        if tp is not None:
            try:
                lw = list(lhsT.ap)[0][0]
                lbase = lhsT.offset // lw if lw else 0
            except Exception:
                lbase = 0
            kw["tile_position"] = (lbase, tp[1])
        r = p.op("tensor", "matmul", inc=("pe", 1) if pinc else None, **kw)
        if pinc:
            bank.ready = r
            for s in slots:
                s.mark = r
        if last:
            bank.started = False
        return r

    def trans(self, bank, out, in_, first, last, waits=()):
        p = self.p
        if first and not bank.started:
            for eng, v in bank.free.items():
                p.wait("tensor", ENGSEM[eng], v)
            bank.free = {}
            bank.started = True
        for sem, v in waits:
            p.wait("tensor", sem, v)
        r = p.op("tensor", "transpose", inc=("pe", 1) if last else None,
                 out=out, in_=in_, identity=self.c_ident)
        if last:
            bank.ready = r
            bank.started = False
        return r

    def ev(self, eng, method, banks=(), waits=(), inc=True, **kwargs):
        p = self.p
        for b in banks:
            p.wait(eng, "pe", b.ready)
        for sem, v in waits:
            p.wait(eng, sem, v)
        sem = ENGSEM[eng]
        r = p.op(eng, method, inc=(sem, 1) if inc else None, **kwargs)
        for b in banks:
            b.free[eng] = r if r is not None else p.cnt[sem]
        return r

    def zw(self, tau):
        return [("dve", self.zv[tau])]

    def fence(self, eng, others=("pe", "dve", "act")):
        for sem in others:
            if sem == ENGSEM.get(eng):
                continue
            self.p.wait(eng, sem, self.p.cnt[sem])

    def barrier(self):
        for eng in ("vector", "scalar", "tensor"):
            self.fence(eng)
            self.p.wait(eng, "h", self.p.cnt["h"])
            for gs in ("g", "g0", "g1", "g2", "g3", "g4", "g5"):
                self.p.wait(eng, gs, self.p.cnt[gs])

    def load_slice(self, slot, hbm, j0, width):
        self.p.wait("gpsimd", "pe", slot.mark)
        src = hbm[:, j0:j0 + width].rearrange("(c p) n -> p c n", p=128)
        dst = slot.ap.rearrange("p (c n) -> p c n", c=NCH)
        slot.ready = self.p.dma("gpsimd", dst, src, slot.sem)


def fm(a, c, t0, t1):
    return a[:, c * T + t0: c * T + t1]


def tmv(a, tc, j0, j1):
    return a[:, tc * D + j0: tc * D + j1]


def emit_ln(ctx, dst):
    p, t = ctx.p, ctx.t
    xs = t["xs"]
    pA, pB = ctx.banks["pA"], ctx.banks["pB"]
    pS1, pS2 = ctx.banks["pS1"], ctx.banks["pS2"]
    R1 = t["R1"]
    ctx.fence("scalar")
    p.wait("vector", "act", p.cnt["act"])
    xsv = p.cnt["dve"]
    apv = [None] * NTAU
    pemk = [0, 0]

    def stats(tau):
        par = tau % 2
        t0 = tau * 512
        b0 = par * 32
        SUM = R1[b0:b0 + 1, 0:512]
        VAR = R1[b0:b0 + 1, 512:1024]
        MNEG = R1[b0:b0 + 1, 1024:1536]
        TMP = R1[b0:b0 + 1, 1536:2048]
        RBm = t["A2"][0:1, 14336 + par * 1024: 14848 + par * 1024]
        RBr = t["A2"][0:1, 14848 + par * 1024: 15360 + par * 1024]
        for c in range(NCH):
            cp = c % 2
            xbs = t["xsc"][:, cp * 512:(cp + 1) * 512]
            sqs = t["xsc"][:, 1024 + cp * 512: 1024 + (cp + 1) * 512]
            cv = ctx.ev("scalar", "copy",
                        waits=[("pe", pemk[cp]), ("dve", xsv)],
                        out=xbs, in_=fm(xs, c, t0, t0 + 512))
            sv = ctx.ev("scalar", "activation",
                        out=sqs, in_=xbs, func=AF.Square)
            ctx.mm(pA, pA.t[0:1, 0:512], t["onesb"][:, 0:1], xbs,
                   first=(c == 0), last=(c == NCH - 1), pinc=True,
                   waits=[("act", cv)])
            ctx.mm(pB, pB.t[0:1, 0:512], t["onesb"][:, 0:1], sqs,
                   first=(c == 0), last=(c == NCH - 1), pinc=True,
                   waits=[("act", sv)])
            pemk[cp] = p.cnt["pe"]
        sev = ctx.ev("scalar", "copy", banks=[pA], out=SUM,
                     in_=pA.t[0:1, 0:512])
        sev = ctx.ev("scalar", "copy", banks=[pB], out=VAR,
                     in_=pB.t[0:1, 0:512])
        ctx.ev("vector", "tensor_scalar", waits=[("act", sev)],
               out=MNEG, in0=SUM,
               scalar1=-1.0 / D, scalar2=None, op0=ALU.mult)
        ctx.ev("vector", "tensor_scalar", out=VAR, in0=VAR,
               scalar1=1.0 / D, scalar2=None, op0=ALU.mult)
        ctx.ev("vector", "tensor_tensor", out=TMP, in0=MNEG, in1=MNEG,
               op=ALU.mult)
        dv = ctx.ev("vector", "tensor_tensor", out=VAR, in0=VAR, in1=TMP,
                    op=ALU.subtract)
        ctx.ev("scalar", "activation", waits=[("dve", dv)],
               out=TMP, in_=VAR, func=AF.Ln,
               bias=t["epsc"][b0:b0 + 1, 0:1])
        av = ctx.ev("scalar", "activation", out=VAR, in_=TMP, func=AF.Exp,
                    scale=-0.5)
        ctx.ev("vector", "tensor_copy", out=RBm, in_=MNEG)
        bv = ctx.ev("vector", "tensor_copy", waits=[("act", av)],
                    out=RBr, in_=VAR)
        apv[tau] = (RBm, RBr, bv)

    def apply(tau):
        par = tau % 2
        t0 = tau * 512
        RBm, RBr, bv = apv[tau]
        bm = pS1 if par == 0 else pS2
        ctx.mm(bm, bm.t[:, 0:512], t["onesrX"][0:1, :], RBm,
               first=True, last=True, waits=[("dve", bv)])
        ctx.mm(pTf_ln, pTf_ln.t[:, 0:512], t["onesrX"][0:1, :], RBr,
               first=True, last=True)
        rsb = t["A2"][:, 12288 + par * 512: 12800 + par * 512]
        ctx.ev("vector", "tensor_copy", banks=[pTf_ln], out=rsb,
               in_=pTf_ln.t[:, 0:512])
        p.wait("vector", "pe", p.cnt["pe"])
        for c in range(NCH):
            tsb = t["SCRF"][:, (c % 2) * 256:(c % 2) * 256 + 256].bitcast(BF16)
            ctx.ev("vector", "tensor_tensor",
                   banks=[bm] if c == NCH - 1 else (),
                   waits=[("pe", bm.ready)],
                   out=tsb, in0=fm(xs, c, t0, t0 + 512),
                   in1=bm.t[:, 0:512], op=ALU.add)
            ctx.ev("vector", "tensor_tensor",
                   out=fm(dst, c, t0, t0 + 512), in0=tsb,
                   in1=rsb, op=ALU.mult)

    pTf_ln = ctx.banks["pTf"]
    ctx.zv = [0] * NTAU
    for tau in range(NTAU + 1):
        if tau < NTAU:
            stats(tau)
        if tau >= 1:
            apply(tau - 1)
            ctx.zv[tau - 1] = p.cnt["dve"]


def emit_moe_ffn(ctx, w):
    p, t = ctx.p, ctx.t
    xs = t["xs"]
    A1, A2, A3 = t["A1"], t["A2"], t["A3"]
    pA, pB, pC, pD = (ctx.banks[k] for k in ("pA", "pB", "pC", "pD"))
    pS1, pS2 = ctx.banks["pS1"], ctx.banks["pS2"]
    P13 = t["slotP"]
    S2s = t["slotS"]

    def w13_of(e):
        return w["w13"][e] if e < NE else w["sw13"]

    def w2_of(e):
        return w["w2"][e] if e < NE else w["sw2"]

    ctx.load_slice(P13[0], w13_of(0), 0, 256)
    emit_ln(ctx, A1)

    gts = t["gts"]
    ctx.fence("vector")
    lgs = (t["sc8x"], t["sc8y"])
    for tc in range(NTC):
        br = pS1 if tc % 2 == 0 else pS2
        for c in range(NCH):
            ctx.mm(br, br.t[:, 0:NE],
                   fm(A1, c, tc * 128, tc * 128 + 128),
                   t["rwt"][:, c * NE:(c + 1) * NE],
                   first=(c == 0), last=(c == NCH - 1),
                   waits=ctx.zw(tc // 4))
        lg = lgs[tc % 2]
        lv = ctx.ev("scalar", "copy", banks=[br], out=lg,
                    in_=br.t[:, 0:NE])
        mx = t["sc1a"]
        ctx.ev("vector", "reduce_max", waits=[("act", lv)],
               out=mx, in_=lg, axis=AX.X, negate=True)
        ex = t["sc8b"]
        sm = t["sc1b"]
        av = ctx.ev("scalar", "activation", waits=[("dve", p.cnt["dve"])],
                    out=ex, in_=lg, func=AF.Exp, bias=mx[:, 0:1],
                    accum_out=sm[:, 0:1])
        rs = t["sc1a"]
        ctx.ev("vector", "reciprocal", waits=[("act", av)], out=rs, in_=sm)
        sc = t["sc8a"]
        ctx.ev("vector", "tensor_scalar", out=sc, in0=ex,
               scalar1=rs[:, 0:1], scalar2=None, op0=ALU.mult)
        sb = t["sc8z"]
        ctx.ev("vector", "tensor_tensor", out=sb, in0=sc, in1=t["ebbct"],
               op=ALU.add)
        m1 = t["sc1b"]
        ctx.ev("vector", "reduce_max", out=m1, in_=sb, axis=AX.X)
        eq = t["sc8c"]
        ctx.ev("vector", "tensor_scalar", out=eq, in0=sb,
               scalar1=m1[:, 0:1], scalar2=1e30, op0=ALU.is_ge, op1=ALU.mult)
        ctx.ev("vector", "tensor_tensor", out=eq, in0=sb, in1=eq,
               op=ALU.subtract)
        m2 = t["sc1a"]
        ctx.ev("vector", "reduce_max", out=m2, in_=eq, axis=AX.X)
        msk = t["sc8c"]
        ctx.ev("vector", "tensor_scalar", out=msk, in0=sb,
               scalar1=m2[:, 0:1], scalar2=None, op0=ALU.is_ge)
        ctx.ev("vector", "tensor_tensor", out=sc, in0=sc, in1=msk,
               op=ALU.mult)
        ctx.ev("vector", "tensor_scalar", out=gts[:, tc * NE:(tc + 1) * NE],
               in0=sc, scalar1=0.5, scalar2=None, op0=ALU.mult)

    gate_bcs = (A2[:, 0:2048], A2[:, 2048:4096])
    e1s = (A2[:, 8192:8704], A2[:, 8704:9216])
    e3s = (A2[:, 9216:9728], A2[:, 9728:10240])
    grows = (A2[0:1, 14336:16384], A2[0:1, 10240:12288])
    gmark = [0, 0]

    def gates(e):
        par = e % 2
        grow = grows[par]
        gate_bc = gate_bcs[par]
        gdv = p.cnt["dve"]
        for tcq in range(4):
            for k in range(4):
                tc = tcq * 4 + k
                ctx.mm(pS1, pS1.t[0:1, k * 128:(k + 1) * 128],
                       gts[:, tc * NE + e: tc * NE + e + 1],
                       ctx.c_ident, first=True, last=True,
                       pinc=(k == 3), waits=[("dve", gdv)])
            ctx.ev("vector", "tensor_copy", banks=[pS1],
                   out=grow[0:1, tcq * 512:(tcq + 1) * 512],
                   in_=pS1.t[0:1, 0:512])
        gv = p.cnt["dve"]
        for tau in range(NTAU):
            ctx.mm(pS2, pS2.t[:, 0:512], t["onesrX"][0:1, :],
                   grow[0:1, tau * 512:(tau + 1) * 512],
                   first=True, last=True, waits=[("dve", gv)])
            ctx.ev("vector", "tensor_copy", banks=[pS2],
                   waits=[("dve", gmark[par])],
                   out=gate_bc[:, tau * 512:(tau + 1) * 512],
                   in_=pS2.t[:, 0:512])

    gates(0)
    for e in range(NE + 1):
        gate_bc = gate_bcs[e % 2]
        for hc in range(NCH):
            s13 = P13[hc % 2]
            if hc == 4 and e + 1 < NE:
                gates(e + 1)
            if hc + 1 < NCH:
                ctx.load_slice(P13[(hc + 1) % 2], w13_of(e),
                               (hc + 1) * 256, 256)
            elif e < NE:
                ctx.load_slice(P13[0], w13_of(e + 1), 0, 256)
            for tau in range(NTAU):
                t0 = tau * 512
                bh1 = pA if tau % 2 == 0 else pC
                bh3 = pB if tau % 2 == 0 else pD
                for c in range(NCH):
                    ctx.mm(bh1, bh1.t[:, 0:512], s13.tile(c, 0),
                           fm(A1, c, t0, t0 + 512),
                           first=(c == 0), last=(c == NCH - 1), slots=[s13],
                           waits=ctx.zw(tau))
                for c in range(NCH):
                    ctx.mm(bh3, bh3.t[:, 0:512], s13.tile(c, 128),
                           fm(A1, c, t0, t0 + 512),
                           first=(c == 0), last=(c == NCH - 1), slots=[s13],
                           waits=ctx.zw(tau))
                eo = e1s[tau % 2]
                e3 = e3s[tau % 2]
                sv = ctx.ev("scalar", "activation", banks=[bh1],
                            waits=[("dve", p.cnt["dve"])],
                            out=eo, in_=bh1.t[:, 0:512], func=AF.Silu)
                ctx.ev("vector", "tensor_copy", banks=[bh3],
                       out=e3, in_=bh3.t[:, 0:512])
                gdst = A3[:, hc * T + t0: hc * T + t0 + 512]
                ctx.ev("vector", "tensor_tensor", waits=[("act", sv)],
                       out=gdst, in0=eo, in1=e3, op=ALU.mult)
                if e < NE:
                    gmark[e % 2] = ctx.ev(
                        "vector", "tensor_tensor", out=gdst, in0=gdst,
                        in1=gate_bc[:, t0:t0 + 512], op=ALU.mult)
                else:
                    ctx.ev("vector", "tensor_scalar", out=gdst, in0=gdst,
                           scalar1=0.5, scalar2=None, op0=ALU.mult)
        gready = p.cnt["dve"]
        for dc in range(NCH):
            s2 = S2s[dc % 2]
            if dc == 0:
                ctx.load_slice(s2, w2_of(e), 0, 128)
            if dc + 1 < NCH:
                ctx.load_slice(S2s[(dc + 1) % 2], w2_of(e),
                               (dc + 1) * 128, 128)
            for tau in range(NTAU):
                t0 = tau * 512
                by = pS1 if tau % 2 == 0 else pS2
                for hc in range(NCH):
                    ctx.mm(by, by.t[:, 0:512], s2.tile(hc),
                           A3[:, hc * T + t0: hc * T + t0 + 512],
                           first=(hc == 0), last=(hc == NCH - 1),
                           slots=[s2], waits=[("dve", gready)])
                ctx.ev("vector", "tensor_tensor", banks=[by],
                       out=fm(xs, dc, t0, t0 + 512),
                       in0=fm(xs, dc, t0, t0 + 512),
                       in1=by.t[:, 0:512], op=ALU.add)


def emit_l1ffn(ctx, w):
    p, t = ctx.p, ctx.t
    xs = t["xs"]
    A1 = t["A1"]
    pA, pB = ctx.banks["pA"], ctx.banks["pB"]
    S = t["slotS"]
    ctx.load_slice(S[0], w["cbt"], 0, 128)
    emit_ln(ctx, A1)
    for nc_ in range(NCH):
        s0 = S[nc_ % 2]
        if nc_ + 1 < NCH:
            ctx.load_slice(S[(nc_ + 1) % 2], w["cbt"], (nc_ + 1) * 128, 128)
        for tau in range(NTAU):
            t0 = tau * 512
            b = pA if tau % 2 == 0 else pB
            for c in range(NCH):
                ctx.mm(b, b.t[:, 0:512], s0.tile(c),
                       fm(A1, c, t0, t0 + 512),
                       first=(c == 0), last=(c == NCH - 1), slots=[s0],
                       waits=ctx.zw(tau))
            tscr = (t["SCRF"][:, 0:512] if tau % 2 == 0
                    else t["SCRF"][:, 512:1024])
            av = ctx.ev("scalar", "activation", banks=[b],
                        waits=[("dve", p.cnt["dve"])],
                        out=tscr, in_=b.t[:, 0:512], func=AF.Identity,
                        bias=t["l1bh"][:, nc_:nc_ + 1],
                        scale=t["scbh"][:, nc_:nc_ + 1])
            ctx.ev("vector", "tensor_tensor", waits=[("act", av)],
                   out=fm(xs, nc_, t0, t0 + 512),
                   in0=fm(xs, nc_, t0, t0 + 512), in1=tscr, op=ALU.add)


def emit_attn1(ctx, w):
    p, t = ctx.p, ctx.t
    xs = t["xs"]
    A1, A2, A3, A4 = t["A1"], t["A2"], t["A3"], t["A4"]
    pA, pB, pC, pD = (ctx.banks[k] for k in ("pA", "pB", "pC", "pD"))
    pS1, pS2 = ctx.banks["pS1"], ctx.banks["pS2"]
    pTf = ctx.banks["pTf"]
    R1 = t["R1"]
    P13 = t["slotP"]
    S = t["slotS"]

    ctx.load_slice(S[0], w["wv"], 0, 128)
    emit_ln(ctx, A1)

    for jb in range(NCH):
        s0 = S[jb % 2]
        if jb + 1 < NCH:
            ctx.load_slice(S[(jb + 1) % 2], w["wv"], (jb + 1) * 128, 128)
        elif True:
            ctx.load_slice(S[2], w["wo"], 0, 128)
        for tc in range(NTC):
            b = pA if tc % 2 == 0 else pB
            for c in range(NCH):
                ctx.mm(b, b.t[:, 0:128],
                       fm(A1, c, tc * 128, tc * 128 + 128),
                       s0.tile(c),
                       first=(c == 0), last=(c == NCH - 1), slots=[s0],
                       waits=ctx.zw(tc // 4))
            ctx.ev("vector", "tensor_copy", banks=[b],
                   out=tmv(A3, tc, jb * 128, jb * 128 + 128),
                   in_=b.t[:, 0:128])

    qp = A4[:, 0:2048]
    kp = A4[:, 2048:4096]
    Pscr = [A4[:, 4096 + i * 512: 4096 + (i + 1) * 512] for i in range(4)]
    rbrow = (A4[0:1, 6144:6656], A4[64:65, 6144:6656])

    ctx.load_slice(P13[0], w["wqk"], 0, 256)
    for hp in range(NH // 2):
        s13 = P13[hp % 2]
        if hp + 1 < NH // 2:
            ctx.load_slice(P13[(hp + 1) % 2], w["wqk"], (hp + 1) * 256, 256)
        for tau in range(NTAU):
            t0 = tau * 512
            for c in range(NCH):
                ctx.mm(pA, pA.t[:, 0:512], s13.tile(c, 0),
                       fm(A1, c, t0, t0 + 512),
                       first=(c == 0), last=(c == NCH - 1), slots=[s13],
                       waits=ctx.zw(tau))
            ctx.ev("vector", "tensor_copy", banks=[pA],
                   out=qp[:, t0:t0 + 512], in_=pA.t[:, 0:512])
            for c in range(NCH):
                ctx.mm(pB, pB.t[:, 0:512], s13.tile(c, 128),
                       fm(A1, c, t0, t0 + 512),
                       first=(c == 0), last=(c == NCH - 1), slots=[s13],
                       waits=ctx.zw(tau))
            ctx.ev("vector", "tensor_copy", banks=[pB],
                   out=kp[:, t0:t0 + 512], in_=pB.t[:, 0:512])
        qkv = p.cnt["dve"]
        pmark = [0, 0, 0, 0]

        def norm_chain(nqt):
            rbv = [0, 0]
            for h01 in range(2):
                b0 = h01 * 64
                bz = pS1 if h01 == 0 else pS2
                zrow = R1[b0:b0 + 1, 0:512]
                lrow = R1[b0:b0 + 1, 512:1024]
                ctx.ev("vector", "tensor_copy", banks=[bz],
                       out=zrow, in_=bz.t[b0:b0 + 1, 0:512])
                dv = p.cnt["dve"]
                ctx.ev("scalar", "activation", waits=[("dve", dv)],
                       out=lrow, in_=zrow, func=AF.Ln)
                av = ctx.ev("scalar", "activation", out=zrow, in_=lrow,
                            func=AF.Exp, scale=-1.0)
                rbv[h01] = ctx.ev("vector", "tensor_copy",
                                  waits=[("act", av)],
                                  out=rbrow[h01], in_=zrow)
            return rbv

        def norm_apply(nqt, rbv):
            t0 = nqt * 512
            for h01 in range(2):
                b0 = h01 * 64
                bb = pS1 if h01 == 0 else pS2
                ctx.mm(bb, bb.t[b0:b0 + 64, 0:512],
                       t["onesrX"][b0:b0 + 1, 0:64], rbrow[h01],
                       first=True, last=True,
                       tp=(0, b0), waits=[("dve", rbv[h01])])
            for h01 in range(2):
                b0 = h01 * 64
                bb = pS1 if h01 == 0 else pS2
                seg = A2[b0:b0 + 64, hp * T + t0: hp * T + t0 + 512]
                ctx.ev("vector", "tensor_tensor", banks=[bb],
                       out=seg, in0=seg, in1=bb.t[b0:b0 + 64, 0:512],
                       op=ALU.mult)


        pend = None
        for qt in range(NTAU):
            t0 = qt * 512
            ntile = 4 * qt + 4
            def st(h01, Ti):
                b = pA if h01 == 0 else pB
                j = Ti - 4 * qt
                r0 = 128 * j if j > 0 else 0
                lhs = kp[h01 * 64:(h01 + 1) * 64, Ti * 128:(Ti + 1) * 128]
                rhs = qp[h01 * 64:(h01 + 1) * 64, t0 + r0:t0 + 512]
                ctx.mm(b, b.t[:, r0:512], lhs, rhs,
                       first=True, last=(j < 0), waits=[("dve", qkv)])
                if j >= 0:
                    ctx.mm(b, b.t[:, r0:r0 + 128], ctx.c_ident,
                           t["mbig"][:, 0:128],
                           first=False, last=True)

            st(0, 0)
            st(1, 0)
            if pend is not None:
                norm_apply(qt - 1, pend)
                pend = None
            for Ti in range(ntile):
                last_t = (Ti == ntile - 1)
                for h01 in range(2):
                    b = pA if h01 == 0 else pB
                    bc = pC if h01 == 0 else pD
                    bz = pS1 if h01 == 0 else pS2
                    b0 = h01 * 64
                    h = hp * 2 + h01
                    Pz = Pscr[(Ti % 2) * 2 + h01]
                    j = Ti - 4 * qt
                    r0 = 128 * j if j > 0 else 0
                    evx = ctx.ev("scalar", "activation", banks=[b],
                                 waits=[("pe", pmark[(Ti % 2) * 2 + h01])],
                                 out=Pz[:, r0:512], in_=b.t[:, r0:512],
                                 func=AF.Exp, scale=0.125)
                    vsl = tmv(A3, Ti, h * 64, h * 64 + 64)
                    ctx.mm(bc, bc.t[b0:b0 + 64, r0:512],
                           vsl, Pz[:, r0:512], first=(Ti == 0), last=last_t,
                           pinc=False, tp=(0, b0), waits=[("act", evx)])
                    ctx.mm(bz, bz.t[b0:b0 + 1, r0:512],
                           t["onesb"][:, 0:1], Pz[:, r0:512],
                           first=(Ti == 0), last=last_t,
                           pinc=True, tp=(0, b0))
                    pmark[(Ti % 2) * 2 + h01] = p.cnt["pe"]
                    if not last_t:
                        st(h01, Ti + 1)

            for bb in (pC, pD, pS1, pS2):
                bb.ready = p.cnt["pe"]
                bb.started = False
            pend = norm_chain(qt)
            ctx.ev("vector", "tensor_copy", banks=[pC],
                   out=A2[0:64, hp * T + t0: hp * T + t0 + 512],
                   in_=pC.t[0:64, 0:512])
            ctx.ev("vector", "tensor_copy", banks=[pD],
                   out=A2[64:128, hp * T + t0: hp * T + t0 + 512],
                   in_=pD.t[64:128, 0:512])

        norm_apply(NTAU - 1, pend)
    ctx.barrier()
    for jc in range(NCH):
        s0 = S[2 + jc % 2]
        if jc + 1 < NCH:
            ctx.load_slice(S[2 + (jc + 1) % 2], w["wo"], (jc + 1) * 128, 128)
        for tau in range(NTAU):
            t0 = tau * 512
            b = pA if tau % 2 == 0 else pB
            for c in range(NCH):
                ctx.mm(b, b.t[:, 0:512], s0.tile(c),
                       fm(A2, c, t0, t0 + 512),
                       first=(c == 0), last=(c == NCH - 1), slots=[s0])
            tscr = (t["SCRF"][:, 0:512] if tau % 2 == 0
                    else t["SCRF"][:, 512:1024])
            av = ctx.ev("scalar", "activation", banks=[b],
                        waits=[("dve", p.cnt["dve"])],
                        out=tscr, in_=b.t[:, 0:512], func=AF.Identity,
                        bias=t["boc"][:, jc:jc + 1])
            ctx.ev("vector", "tensor_tensor", waits=[("act", av)],
                   out=fm(xs, jc, t0, t0 + 512),
                   in0=fm(xs, jc, t0, t0 + 512), in1=tscr, op=ALU.add)


def emit_attn0(ctx, w):
    p, t = ctx.p, ctx.t
    xs = t["xs"]
    A1, A2, A3, A4 = t["A1"], t["A2"], t["A3"], t["A4"]
    pA, pB, pC, pD = (ctx.banks[k] for k in ("pA", "pB", "pC", "pD"))
    pS1 = ctx.banks["pS1"]
    pT = ctx.banks["pT"]
    pTf = ctx.banks["pTf"]
    S = t["slotS"]
    qkvT = w["qkvT"]

    ctx.load_slice(S[0], qkvT, D, 128)
    phases = int(os.environ.get("ATTN0_PHASES", "9"))
    scanp = int(os.environ.get("ATTN0_SCAN", "9"))
    emit_ln(ctx, A1)
    if phases < 1:
        return

    kmx = t["kmx"]
    k4 = t["k4"]
    for kc in range(NCH):
        s0 = S[kc % 2]
        if kc + 1 < NCH:
            ctx.load_slice(S[(kc + 1) % 2], qkvT, D + (kc + 1) * 128, 128)
        else:
            ctx.load_slice(S[2], qkvT, 0, 128)
        for tau in range(NTAU):
            b = pA if tau % 2 == 0 else pB
            for c in range(NCH):
                ctx.mm(b, b.t[:, 0:512], s0.tile(c),
                       fm(A1, c, tau * 512, tau * 512 + 512),
                       first=(c == 0), last=(c == NCH - 1), slots=[s0],
                       waits=ctx.zw(tau))
            ctx.ev("vector", "reduce_max", banks=[b],
                   out=k4[:, tau:tau + 1], in_=b.t[:, 0:512], axis=AX.X)
        ctx.ev("vector", "reduce_max", out=kmx[:, kc:kc + 1], in_=k4,
               axis=AX.X, negate=True)
    kmxv = p.cnt["dve"]

    S_f = t["SCRF"][:, 512:1024]
    ctx.ev("vector", "memset", ap=S_f, constant=0.0)
    sprev = t["R1"][0:1, 0:D]
    ctx.ev("vector", "memset", ap=sprev, constant=1e-9)

    for hf in range(2):
        hb = hf * 1024
        other = 1 - hf
        sb_ = A2[:, other * 8192: other * 8192 + 8192]
        ketm = sb_[:, 0:1024]
        qh = sb_[:, 1024:2048]
        qhf = sb_[:, 2048:3072]
        am = sb_[:, 3072:3328]
        S_b = sb_[:, 3328:3840]
        if hf == 0:
            ctx.ev("vector", "memset", ap=S_b, constant=0.0)

        if phases < 2:
            continue
        qev = 0
        for jb in range(NCH):
            s0 = S[2 + jb % 2]
            if jb + 1 < NCH:
                ctx.load_slice(S[2 + (jb + 1) % 2], qkvT,
                               (jb + 1) * 128, 128)
            else:
                ctx.load_slice(S[2], qkvT, 2 * D, 128)
            for tc2 in range(NTC // 2):
                tc = hf * 8 + tc2
                b = pA if tc2 % 2 == 0 else pB
                for c in range(NCH):
                    ctx.mm(b, b.t[:, 0:128],
                           fm(A1, c, tc * 128, tc * 128 + 128),
                           s0.tile(c),
                           first=(c == 0), last=(c == NCH - 1), slots=[s0],
                           waits=ctx.zw(tc // 4))
                qe = t["qe"]
                av = ctx.ev("scalar", "activation", banks=[b],
                            waits=[("dve", qev)],
                            out=qe, in_=b.t[:, 0:128], func=AF.Exp)
                ctx.ev("vector", "reduce_sum", waits=[("act", av)],
                       out=t["sc8a"][:, 0:2],
                       in_=qe.rearrange("p (h c) -> p h c", c=HS),
                       axis=AX.X)
                ctx.ev("vector", "reciprocal", out=t["sc8b"][:, 0:2],
                       in_=t["sc8a"][:, 0:2])
                ctx.ev("vector", "tensor_scalar", out=t["sc8b"][:, 0:2],
                       in0=t["sc8b"][:, 0:2], scalar1=0.125, scalar2=None,
                       op0=ALU.mult)
                for h2 in range(2):
                    qev = ctx.ev("vector", "tensor_scalar",
                                 out=tmv(A2, tc, jb * 128 + h2 * 64,
                                         jb * 128 + h2 * 64 + 64),
                                 in0=qe[:, h2 * 64:(h2 + 1) * 64],
                                 scalar1=t["sc8b"][:, h2:h2 + 1],
                                 scalar2=None, op0=ALU.mult)

        if phases < 3:
            continue
        for kc in range(NCH):
            s0 = S[kc % 2]
            if kc == 0:
                ctx.load_slice(s0, qkvT, D, 128)
            if kc + 1 < NCH:
                ctx.load_slice(S[(kc + 1) % 2], qkvT, D + (kc + 1) * 128, 128)
            for tau2 in range(2):
                b = pA if tau2 % 2 == 0 else pB
                tb = hb + tau2 * 512
                for c in range(NCH):
                    ctx.mm(b, b.t[:, 0:512], s0.tile(c),
                           fm(A1, c, tb, tb + 512),
                           first=(c == 0), last=(c == NCH - 1), slots=[s0],
                           waits=ctx.zw(tb // 512))
                ctx.ev("scalar", "activation", banks=[b],
                       waits=[("dve", kmxv)],
                       out=fm(A3, kc, tb, tb + 512), in_=b.t[:, 0:512],
                       func=AF.Exp, bias=kmx[:, kc:kc + 1])

        if phases < 4:
            continue
        for jb in range(NCH):
            s0 = S[2 + jb % 2]
            if jb + 1 < NCH:
                ctx.load_slice(S[2 + (jb + 1) % 2], qkvT,
                               2 * D + (jb + 1) * 128, 128)
            for tc2 in range(NTC // 2):
                tc = hf * 8 + tc2
                b = pA if tc2 % 2 == 0 else pB
                for c in range(NCH):
                    ctx.mm(b, b.t[:, 0:128],
                           fm(A1, c, tc * 128, tc * 128 + 128),
                           s0.tile(c),
                           first=(c == 0), last=(c == NCH - 1), slots=[s0],
                           waits=ctx.zw(tc // 4))
                ctx.ev("vector", "tensor_copy", banks=[b],
                       out=A4[:, tc2 * D + jb * 128: tc2 * D + jb * 128 + 128],
                       in_=b.t[:, 0:128])
        ctx.barrier()

        if phases < 5:
            continue
        kexp_act = p.cnt["act"]
        qsm_dve = p.cnt["dve"]
        qtrans_prev = 0
        for ck2 in range(8):
            ck = hf * 8 + ck2
            cb = ck * 128
            for c in range(NCH):
                ctx.trans(pT, pT.t[:, c * 128:(c + 1) * 128],
                          fm(A3, c, cb, cb + 128),
                          first=(c == 0), last=(c == NCH - 1),
                          waits=[("act", kexp_act)])
            ctx.ev("vector", "tensor_copy", banks=[pT], out=ketm,
                   in_=pT.t[:, 0:1024])
            kv = p.cnt["dve"]
            if scanp < 1:
                continue
            for half in range(2):
                b = pA if half == 0 else pB
                ctx.mm(b, b.t[:, 0:512], t["lte2"][:, 0:128],
                       ketm[:, half * 512:(half + 1) * 512],
                       first=True, last=False, waits=[("dve", kv)])
                ctx.mm(b, b.t[:, 0:512], t["onesrf"][0:1, :],
                       sprev[0:1, half * 512:(half + 1) * 512],
                       first=False, last=True)
            for half in range(2):
                b = pA if half == 0 else pB
                lnz = t["R1"][:, 1024:1536]
                rz = t["R1"][:, 1536:2048]
                ctx.ev("scalar", "activation", banks=[b],
                       waits=[("dve", p.cnt["dve"])], out=lnz,
                       in_=b.t[:, 0:512], func=AF.Ln)
                av = ctx.ev("scalar", "activation", out=rz, in_=lnz,
                            func=AF.Exp, scale=-1.0)
                ctx.ev("vector", "tensor_tensor",
                       waits=[("act", av), ("pe", qtrans_prev)],
                       out=qh[:, half * 512:(half + 1) * 512],
                       in0=tmv(A2, ck, half * 512, half * 512 + 512),
                       in1=rz, op=ALU.mult)
            qv = p.cnt["dve"]
            for c in range(NCH):
                ctx.trans(pT, pT.t[:, c * 128:(c + 1) * 128],
                          qh[:, c * 128:(c + 1) * 128],
                          first=(c == 0), last=(c == NCH - 1),
                          waits=[("dve", qv)])
            ctx.ev("vector", "tensor_copy", banks=[pT], out=qhf,
                   in_=pT.t[:, 0:1024])
            qfv = p.cnt["dve"]
            qtrans_prev = pT.ready
            pS2 = ctx.banks["pS2"]
            for half in range(2):
                bcs = pS1 if half == 0 else pS2
                ctx.mm(bcs, bcs.t[0:1, 0:512], t["onesb"][:, 0:1],
                       ketm[:, half * 512:(half + 1) * 512],
                       first=True, last=True, waits=[("dve", kv)])
            svmark = p.cnt["dve"]
            svmark_a = p.cnt["act"]
            if scanp < 3:
                continue
            for hp2 in range(NH // 2):
                he, ho = hp2 * 2, hp2 * 2 + 1
                for h01, h in ((0, he), (1, ho)):
                    ba = pA if h01 == 0 else pB
                    lhs = A3[(h % 2) * 64:(h % 2) * 64 + 64,
                             (h // 2) * T + cb:(h // 2) * T + cb + 128]
                    rhs = qhf[(h % 2) * 64:(h % 2) * 64 + 64,
                              (h // 2) * 128:(h // 2) * 128 + 128]
                    ctx.mm(ba, ba.t[:, 0:128], lhs, rhs,
                           first=True, last=True,
                           waits=[("dve", qfv)])
                ctx.ev("vector", "tensor_tensor", banks=[pA],
                       out=am[:, 0:128], in0=pA.t[:, 0:128],
                       in1=t["lte2"][:, 0:128], op=ALU.mult)
                mv = ctx.ev("vector", "tensor_tensor", banks=[pB],
                            out=am[:, 128:256], in0=pB.t[:, 0:128],
                            in1=t["lte2"][:, 128:256], op=ALU.mult)
                if scanp < 4:
                    continue
                for h01, h in ((0, he), (1, ho)):
                    b0 = h01 * 64
                    ctx.mm(pD, pD.t[b0:b0 + 64, 0:128],
                           S_b[b0:b0 + 64, hp2 * 64:(hp2 + 1) * 64],
                           qhf[b0:b0 + 64,
                               (h // 2) * 128:(h // 2) * 128 + 128],
                           first=True, last=False, pinc=False, tp=(0, b0),
                           waits=[("dve", svmark), ("act", svmark_a)])
                    ctx.mm(pD, pD.t[b0:b0 + 64, 0:128],
                           A4[:, ck2 * D + h * 64: ck2 * D + (h + 1) * 64],
                           am[:, h01 * 128:(h01 + 1) * 128],
                           first=False, last=True, pinc=(h01 == 1),
                           tp=(0, b0), waits=[("dve", mv)])
                for h01, h in ((0, he), (1, ho)):
                    b0 = h01 * 64
                    ctx.mm(pTf, pTf.t[b0:b0 + 64, 0:64],
                           ketm[:, h * 64:(h + 1) * 64],
                           A4[:, ck2 * D + h * 64: ck2 * D + (h + 1) * 64],
                           first=True, last=True, pinc=(h01 == 1),
                           tp=(0, b0))
                ctx.ev("scalar", "copy", banks=[pD],
                       out=A1[:, hp2 * T + cb: hp2 * T + cb + 128],
                       in_=pD.t[:, 0:128])
                if scanp < 5:
                    continue
                ctx.ev("vector", "tensor_tensor", banks=[pTf],
                       out=S_f[:, hp2 * 64:(hp2 + 1) * 64],
                       in0=S_f[:, hp2 * 64:(hp2 + 1) * 64],
                       in1=pTf.t[:, 0:64], op=ALU.add)
                ctx.ev("scalar", "copy", waits=[("dve", p.cnt["dve"])],
                       out=S_b[:, hp2 * 64:(hp2 + 1) * 64],
                       in_=S_f[:, hp2 * 64:(hp2 + 1) * 64])
            for half in range(2):
                bcs = pS1 if half == 0 else pS2
                ctx.ev("vector", "tensor_tensor", banks=[bcs],
                       out=sprev[0:1, half * 512:(half + 1) * 512],
                       in0=sprev[0:1, half * 512:(half + 1) * 512],
                       in1=bcs.t[0:1, 0:512], op=ALU.add)
        ctx.barrier()

    if phases < 6:
        return
    ctx.load_slice(S[0], w["outT"], 0, 128)
    for jc in range(NCH):
        s0 = S[jc % 2]
        if jc + 1 < NCH:
            ctx.load_slice(S[(jc + 1) % 2], w["outT"], (jc + 1) * 128, 128)
        for tau in range(NTAU):
            t0 = tau * 512
            b = (pA, pB, pC, pD)[tau]
            for c in range(NCH):
                ctx.mm(b, b.t[:, 0:512], s0.tile(c),
                       fm(A1, c, t0, t0 + 512),
                       first=(c == 0), last=(c == NCH - 1), slots=[s0])
            ctx.ev("vector", "tensor_tensor", banks=[b],
                   out=fm(xs, jc, t0, t0 + 512),
                   in0=fm(xs, jc, t0, t0 + 512),
                   in1=b.t[:, 0:512], op=ALU.add)


class BlockShim:
    def __init__(self, blk):
        self.blk = blk

    def __getattr__(self, name):
        dec = getattr(self.blk, name)

        def call(fn):
            dec(fn)
        return call


def build_program(subids):
    nc = bass.Bass("TRN2", target_bir_lowering=False)
    h = {}
    h["xin"] = nc.dram_tensor("xin", [NSEQ, D, T], F32,
                              kind="ExternalInput").ap()
    h["xout"] = nc.dram_tensor("xout", [NSEQ, D, T], F32,
                               kind="ExternalOutput").ap()

    def din(name, shape, dt=F32):
        h[name] = nc.dram_tensor(name, shape, dt, kind="ExternalInput").ap()

    needs = set(subids)
    if needs & {FFN0A, FFN0B}:
        din("w13", [NE, D, 2 * D]); din("w2", [NE, D, D])
        din("sw13", [D, 2 * D]); din("sw2", [D, D])
        din("rw", [D, NE]); din("ebbc", [128, NE])
    if ATTN0 in needs:
        din("wqkvT", [D, 3 * D]); din("woutT", [D, D])
    if ATTN1 in needs:
        din("wqk", [D, 2 * D]); din("wv", [D, D]); din("wo", [D, D])
        din("bo", [D])
    if needs & {FFN1A, FFN1B}:
        din("cbt", [D, D]); din("scb", [D]); din("l1b", [D])
    din("c_ident", [128, 128], BF16)
    din("c_lte2", [128, 256], BF16)
    din("c_mbig", [128, 128], BF16)

    prog = Prog()
    with ExitStack() as st:
        e = st.enter_context
        t = {}

        def sb(name, shape, dt):
            t[name] = e(nc.sbuf_tensor(name, shape, dt))[
                tuple(slice(None) for _ in shape)]

        sb("xs", [128, NCH * T], F32)
        sb("A1", [128, NCH * T], BF16)
        sb("A2", [128, NCH * T], BF16)
        sb("A3", [128, NCH * T], BF16)
        sb("A4", [128, 8192], BF16)
        sb("wsl", [128, 8192], BF16)
        sb("R1", [128, T], F32)
        sb("SCRF", [128, 1024], F32)
        sb("rwt", [128, NCH * NE], BF16)
        sb("ebbct", [128, NE], F32)
        sb("kmx", [128, NCH], F32)
        sb("k4", [128, 4], F32)
        for nm in ("sc8a", "sc8b", "sc8c", "sc8x", "sc8y", "sc8z"):
            sb(nm, [128, NE], F32)
        for nm in ("sc1a", "sc1b", "epsc"):
            sb(nm, [128, 1], F32)
        sb("boc", [128, NCH], F32)
        sb("scbh", [128, NCH], F32)
        sb("l1bh", [128, NCH], F32)
        sb("onesb", [128, 1], BF16)
        sb("onesrX", [128, 128], BF16)
        sb("onesrf", [1, 128], F32)
        sb("identb", [128, 128], BF16)
        sb("lte2", [128, 256], BF16)
        sb("mbig", [128, 128], BF16)

        t["gts"] = t["A4"][:, 0:NTC * NE]
        t["qe"] = t["A4"][:, 256:384]
        t["xsc"] = t["A4"][:, 2048:4096]

        banks = {}
        pab = e(nc.psum_tensor("pAB", [128, 1024], F32))[:, :]
        t["pAB"] = pab
        banks["pA"] = Bank(pab[:, 0:512])
        banks["pB"] = Bank(pab[:, 512:1024])
        for nm in ("pC", "pD", "pS1", "pS2"):
            banks[nm] = Bank(e(nc.psum_tensor(nm, [128, 512], F32))[:, :])
        banks["pT"] = Bank(e(nc.psum_tensor("pT", [128, 1024], BF16))[:, :])
        banks["pTf"] = Bank(e(nc.psum_tensor("pTf", [128, 512], F32))[:, :])

        sems = {nm: e(nc.semaphore("s_" + nm))
                for nm in ("pe", "dve", "act", "g", "h",
                           "g0", "g1", "g2", "g3", "g4", "g5")}

        ctx = Ctx(nc, prog)
        ctx.t = t
        ctx.banks = banks
        ctx.c_ident = t["identb"]
        t["slotP"] = [Slot(t["wsl"][:, 0:2048], 256, "g0"),
                      Slot(t["wsl"][:, 2048:4096], 256, "g1")]
        t["slotS"] = [Slot(t["wsl"][:, 4096 + i * 1024:
                                    4096 + (i + 1) * 1024], 128,
                           "g%d" % (2 + i))
                      for i in range(4)]

        p = prog
        p.dma("sync", t["identb"], h["c_ident"][:, :], "h")
        p.dma("sync", t["lte2"], h["c_lte2"][:, :], "h")
        p.dma("sync", t["mbig"], h["c_mbig"][:, :], "h")
        if needs & {FFN0A, FFN0B}:
            p.dma("gpsimd", t["rwt"].rearrange("p (c e) -> p c e", c=NCH),
                  h["rw"].rearrange("(c p) e -> p c e", p=128), "g")
            p.dma("sync", t["ebbct"], h["ebbc"][:, :], "h")
        if ATTN1 in needs:
            bor = h["bo"].rearrange("(c p one) -> c p one", p=128, one=1)
            for c in range(NCH):
                p.dma("sync", t["boc"][:, c:c + 1], bor[c], "h")
        if needs & {FFN1A, FFN1B}:
            scbr = h["scb"].rearrange("(c p one) -> c p one", p=128, one=1)
            l1br = h["l1b"].rearrange("(c p one) -> c p one", p=128, one=1)
            for c in range(NCH):
                p.dma("sync", t["scbh"][:, c:c + 1], scbr[c], "h")
                p.dma("sync", t["l1bh"][:, c:c + 1], l1br[c], "h")
        hload = p.cnt["h"]
        gload = p.cnt["g"]
        for eng in ("vector", "scalar", "tensor"):
            p.wait(eng, "h", hload)
            p.wait(eng, "g", gload)
        prog.op("vector", "memset", ap=t["onesb"], constant=1.0)
        prog.op("vector", "memset", ap=t["epsc"], constant=1e-5)
        prog.op("vector", "memset", ap=t["onesrX"], constant=1.0)
        if needs & {FFN1A, FFN1B}:
            prog.op("vector", "tensor_scalar", out=t["scbh"], in0=t["scbh"],
                    scalar1=1.0 / 254.0, scalar2=None, op0=ALU.mult)
            prog.op("vector", "tensor_scalar", out=t["l1bh"], in0=t["l1bh"],
                    scalar1=0.5, scalar2=None, op0=ALU.mult)
        mv = prog.op("vector", "memset", ap=t["onesrf"], constant=1.0,
                     inc=("dve", 1))
        for eng in ("scalar", "tensor"):
            p.wait(eng, "dve", mv)
        p.wait("gpsimd", "dve", mv)

        for seq in range(NSEQ):
            xr = h["xin"].rearrange("s (c p) t -> s c p t", p=128)
            p.wait("sync", "dve", p.cnt["dve"])
            p.wait("sync", "pe", p.cnt["pe"])
            p.wait("sync", "h", p.cnt["h"])
            for c in range(NCH):
                p.dma("sync", fm(t["xs"], c, 0, T), xr[seq, c], "h")
            hx = p.cnt["h"]
            for eng in ("vector", "scalar", "tensor"):
                p.wait(eng, "h", hx)
            for sid in subids:
                kind = EMITTERS[sid]
                if kind == "ffn0":
                    emit_moe_ffn(ctx, {
                        "w13": [h["w13"][ei] for ei in range(NE)],
                        "w2": [h["w2"][ei] for ei in range(NE)],
                        "sw13": h["sw13"], "sw2": h["sw2"]})
                elif kind == "ffn1":
                    emit_l1ffn(ctx, {"cbt": h["cbt"]})
                elif kind == "attn0":
                    emit_attn0(ctx, {"qkvT": h["wqkvT"], "outT": h["woutT"]})
                elif kind == "attn1":
                    emit_attn1(ctx, {"wqk": h["wqk"], "wv": h["wv"],
                                     "wo": h["wo"]})
                ctx.barrier()
            xo = h["xout"].rearrange("s (c p) t -> s c p t", p=128)
            p.wait("sync", "dve", p.cnt["dve"])
            p.wait("sync", "pe", p.cnt["pe"])
            for c in range(NCH):
                p.dma("sync", xo[seq, c], fm(t["xs"], c, 0, T), "h")
        p.wait("sync", "h", p.cnt["h"])

        blk = e(nc.Block())
        prog.replay(BlockShim(blk), sems)
    return nc


def _consts():
    ident = np.eye(128, dtype=np.float32).astype(BFNP)
    lte = (np.arange(128)[:, None] <= np.arange(128)[None, :])
    lte2 = np.concatenate([lte, lte], axis=1).astype(np.float32).astype(BFNP)
    s = np.arange(128)[:, None]
    u = np.arange(128)[None, :]
    mbig = np.where(s <= u, 0.0, -1e9).astype(np.float32).astype(BFNP)
    return ident, lte2, mbig


def _interleave128(a, b):
    R, N = a.shape
    return np.concatenate([a.reshape(R, N // 128, 128),
                           b.reshape(R, N // 128, 128)],
                          axis=2).reshape(R, 2 * N)


def host_inputs(inputs):
    ident, lte2, mbig = _consts()

    def f32(a):
        return np.ascontiguousarray(np.asarray(a, dtype=np.float32))

    w1 = np.asarray(inputs["L0_ffn_w1"], dtype=np.float32)
    w3 = np.asarray(inputs["L0_ffn_w3"], dtype=np.float32)
    w13 = np.ascontiguousarray(
        np.stack([_interleave128(w1[ei], w3[ei]) for ei in range(NE)]))
    sw13 = np.ascontiguousarray(_interleave128(
        np.asarray(inputs["L0_ffn_sw1"], dtype=np.float32),
        np.asarray(inputs["L0_ffn_sw3"], dtype=np.float32)))
    wqk = np.ascontiguousarray(_interleave128(
        np.asarray(inputs["L1_attn_Wq"], dtype=np.float32),
        np.asarray(inputs["L1_attn_Wk"], dtype=np.float32)))
    common = {
        "w13": w13,
        "w2": f32(inputs["L0_ffn_w2"]),
        "sw13": sw13,
        "sw2": f32(inputs["L0_ffn_sw2"]),
        "rw": f32(inputs["L0_ffn_router_w"]),
        "ebbc": np.ascontiguousarray(np.tile(
            np.asarray(inputs["L0_ffn_expert_bias"],
                       dtype=np.float32)[None, :], (128, 1))),
        "wqkvT": f32(np.asarray(inputs["L0_attn_W_qkv"]).T),
        "woutT": f32(np.asarray(inputs["L0_attn_W_out"]).T),
        "wqk": wqk,
        "wv": f32(inputs["L1_attn_Wv"]),
        "wo": f32(inputs["L1_attn_Wo"]),
        "bo": f32(inputs["L1_attn_bo"]),
        "cbt": f32(np.asarray(inputs["L1_ffn_CB"]).T),
        "scb": f32(inputs["L1_ffn_SCB"]),
        "l1b": f32(inputs["L1_ffn_bias"]),
        "c_ident": ident, "c_lte2": lte2, "c_mbig": mbig,
    }
    return common


USED = {
    "ffn0": ["w13", "w2", "sw13", "sw2", "rw", "ebbc"],
    "attn0": ["wqkvT", "woutT"],
    "attn1": ["wqk", "wv", "wo", "bo"],
    "ffn1": ["cbt", "scb", "l1b"],
}


def kernel(**inputs):
    x = np.asarray(inputs["L0_attn_x"], dtype=np.float32)
    B = x.shape[0]
    percore = B // 8
    xcur = np.ascontiguousarray(
        x.reshape(8, percore, T, D).transpose(0, 1, 3, 2))
    common = host_inputs(inputs)
    for group in LAUNCH_GROUPS:
        names = {"c_ident", "c_lte2", "c_mbig"}
        for sid in group:
            names.update(USED[EMITTERS[sid]])
        nc = build_program(group)
        in_maps = [dict({n: common[n] for n in names}, xin=xcur[c])
                   for c in range(8)]
        res = run_bass_kernel_spmd(nc, in_maps, core_ids=list(range(8)))
        xcur = np.stack([res.results[c]["xout"] for c in range(8)])
    out = np.ascontiguousarray(
        xcur.transpose(0, 1, 3, 2).reshape(B, T, D)).astype(np.float32)
    return out


# version 15
# speedup vs baseline: 1.1994x; 1.0783x over previous
import os
import numpy as np
import ml_dtypes
from contextlib import ExitStack

import concourse.bass as bass
import concourse.mybir as mybir
from concourse.bass_utils import run_bass_kernel_spmd
from concourse.bass_types import AP

F32 = mybir.dt.float32
BF16 = mybir.dt.bfloat16
AF = mybir.ActivationFunctionType
ALU = mybir.AluOpType
AX = mybir.AxisListType
BFNP = ml_dtypes.bfloat16

D = 1024
T = 2048
NCH = 8
NTC = 16
NTAU = 4
NH = 16
HS = 64
NE = 8
NSEQ = 2

FFN0A, ATTN0, FFN0B, FFN1A, ATTN1, FFN1B = range(6)
ALL_SUBS = [FFN0A, ATTN0, FFN0B, FFN1A, ATTN1, FFN1B]
EMITTERS = {FFN0A: "ffn0", ATTN0: "attn0", FFN0B: "ffn0",
            FFN1A: "ffn1", ATTN1: "attn1", FFN1B: "ffn1"}

LAUNCH_GROUPS = [ALL_SUBS]

ENGSEM = {"vector": "dve", "scalar": "act", "tensor": "pe", "gpsimd": "pool"}


def _regions(kwargs):
    reads, writes = [], []
    for k, v in kwargs.items():
        if not isinstance(v, AP):
            continue
        try:
            pairs = list(v.ap)
            w = pairs[0][0]
            off = v.offset
            p0 = off // w if w else 0
            c0 = off % w if w else off
            pcnt = pairs[0][1]
            span = 1
            for stp, cn in pairs[1:]:
                span += (cn - 1) * abs(stp)
            reg = (v.tensor.name, p0, p0 + pcnt, c0, c0 + span)
        except Exception:
            reg = (v.tensor.name, 0, 1 << 30, 0, 1 << 30)
        if k in ("out", "accum_out", "ap"):
            writes.append(reg)
        else:
            reads.append(reg)
    return reads, writes


def _overlap(a, b):
    return (a[0] == b[0] and a[1] < b[2] and b[1] < a[2]
            and a[3] < b[4] and b[3] < a[4])


class Prog:
    ENGS = ("sync", "gpsimd", "vector", "scalar", "tensor")

    def __init__(self):
        self.steps = {e: [] for e in self.ENGS}
        self.cnt = {s: 0 for s in
                    ("pe", "dve", "act", "pool", "g", "h",
                     "g0", "g1", "g2", "g3", "g4", "g5")}
        self.maxwait = {}
        self.dirty = {e: [] for e in self.ENGS}
        self.rdirty = {e: [] for e in self.ENGS}

    def wait(self, eng, sem, val):
        if val <= 0:
            return
        if self.maxwait.get((eng, sem), -1) >= val:
            return
        self.maxwait[(eng, sem)] = val
        self.steps[eng].append(("wait", sem, val))

    def op(self, eng, method, inc=None, **kwargs):
        if eng in ("vector", "scalar"):
            reads, writes = _regions(kwargs)
            need = any(_overlap(r, d) for r in reads for d in self.dirty[eng])
            if not need:
                need = any(_overlap(wr, d) for wr in writes
                           for d in self.dirty[eng] + self.rdirty[eng])
            if need:
                self.steps[eng].append(("drain",))
                self.dirty[eng].clear()
                self.rdirty[eng].clear()
            self.dirty[eng].extend(writes)
            self.rdirty[eng].extend(reads)
        incs = []
        if inc is not None:
            sem, amt = inc if isinstance(inc, tuple) else (inc, 1)
            self.cnt[sem] += amt
            incs.append((sem, amt))
        self.steps[eng].append(("op", method, kwargs, incs))
        return self.cnt[incs[0][0]] if incs else None

    def dma(self, eng, out, in_, sem):
        self.cnt[sem] += 16
        self.steps[eng].append(("op", "dma_start", {"out": out, "in_": in_},
                                [(sem, 16)]))
        return self.cnt[sem]

    def replay(self, block, sems):
        for eng_name in self.ENGS:
            steps = self.steps[eng_name]
            if not steps:
                continue

            def body(eng, _steps=steps):
                for st in _steps:
                    if st[0] == "wait":
                        eng.wait_ge(sems[st[1]], st[2])
                    elif st[0] == "drain":
                        eng.drain()
                    else:
                        _, method, kwargs, incs = st
                        try:
                            ins = getattr(eng, method)(**kwargs)
                        except Exception:
                            import sys
                            print("REPLAY FAIL:", eng_name, method,
                                  {k: (str(v.tensor.name) + str(list(v.ap)) +
                                       f"@{v.offset}")
                                   if isinstance(v, AP) else v
                                   for k, v in kwargs.items()},
                                  file=sys.stderr)
                            raise
                        for sem, amt in incs:
                            ins.then_inc(sems[sem], amt)
            getattr(block, eng_name)(body)


class Bank:
    def __init__(self, tile):
        self.t = tile
        self.ready = 0
        self.free = {}
        self.started = False


class Slot:
    def __init__(self, ap, pitch, sem):
        self.ap = ap
        self.pitch = pitch
        self.sem = sem
        self.ready = 0
        self.mark = 0

    def tile(self, c, off=0, wdt=128):
        return self.ap[:, c * self.pitch + off: c * self.pitch + off + wdt]


class Ctx:
    def __init__(self, nc, prog):
        self.nc = nc
        self.p = prog
        self.t = {}
        self.banks = {}
        self.c_ident = None
        self.zv = [0] * NTAU

    def mm(self, bank, out, lhsT, rhs, first, last, pinc=None, tp=None,
           slots=(), waits=()):
        p = self.p
        if first and not bank.started:
            for eng, v in bank.free.items():
                p.wait("tensor", ENGSEM[eng], v)
            bank.free = {}
            bank.started = True
        for s in slots:
            p.wait("tensor", s.sem, s.ready)
        for sem, v in waits:
            p.wait("tensor", sem, v)
        if pinc is None:
            pinc = last
        kw = dict(out=out, lhsT=lhsT, rhs=rhs, start=first, stop=last)
        if tp is not None:
            try:
                lw = list(lhsT.ap)[0][0]
                lbase = lhsT.offset // lw if lw else 0
            except Exception:
                lbase = 0
            kw["tile_position"] = (lbase, tp[1])
        r = p.op("tensor", "matmul", inc=("pe", 1) if pinc else None, **kw)
        if pinc:
            bank.ready = r
            for s in slots:
                s.mark = r
        if last:
            bank.started = False
        return r

    def trans(self, bank, out, in_, first, last, waits=()):
        p = self.p
        if first and not bank.started:
            for eng, v in bank.free.items():
                p.wait("tensor", ENGSEM[eng], v)
            bank.free = {}
            bank.started = True
        for sem, v in waits:
            p.wait("tensor", sem, v)
        r = p.op("tensor", "transpose", inc=("pe", 1) if last else None,
                 out=out, in_=in_, identity=self.c_ident)
        if last:
            bank.ready = r
            bank.started = False
        return r

    def ev(self, eng, method, banks=(), waits=(), inc=True, **kwargs):
        p = self.p
        for b in banks:
            p.wait(eng, "pe", b.ready)
        for sem, v in waits:
            p.wait(eng, sem, v)
        sem = ENGSEM[eng]
        r = p.op(eng, method, inc=(sem, 1) if inc else None, **kwargs)
        for b in banks:
            b.free[eng] = r if r is not None else p.cnt[sem]
        return r

    def zw(self, tau):
        return [("dve", self.zv[tau])]

    def fence(self, eng, others=("pe", "dve", "act")):
        for sem in others:
            if sem == ENGSEM.get(eng):
                continue
            self.p.wait(eng, sem, self.p.cnt[sem])

    def barrier(self):
        for eng in ("vector", "scalar", "tensor"):
            self.fence(eng)
            self.p.wait(eng, "h", self.p.cnt["h"])
            for gs in ("g", "g0", "g1", "g2", "g3", "g4", "g5"):
                self.p.wait(eng, gs, self.p.cnt[gs])

    def load_slice(self, slot, hbm, j0, width):
        self.p.wait("gpsimd", "pe", slot.mark)
        src = hbm[:, j0:j0 + width].rearrange("(c p) n -> p c n", p=128)
        dst = slot.ap.rearrange("p (c n) -> p c n", c=NCH)
        slot.ready = self.p.dma("gpsimd", dst, src, slot.sem)


def fm(a, c, t0, t1):
    return a[:, c * T + t0: c * T + t1]


def tmv(a, tc, j0, j1):
    return a[:, tc * D + j0: tc * D + j1]


def emit_ln(ctx, dst):
    p, t = ctx.p, ctx.t
    xs = t["xs"]
    pA, pB = ctx.banks["pA"], ctx.banks["pB"]
    pS1, pS2 = ctx.banks["pS1"], ctx.banks["pS2"]
    R1 = t["R1"]
    ctx.fence("scalar")
    p.wait("vector", "act", p.cnt["act"])
    xsv = p.cnt["dve"]
    apv = [None] * NTAU
    pemk = [0, 0]

    def stats(tau):
        par = tau % 2
        t0 = tau * 512
        b0 = par * 32
        SUM = R1[b0:b0 + 1, 0:512]
        VAR = R1[b0:b0 + 1, 512:1024]
        MNEG = R1[b0:b0 + 1, 1024:1536]
        TMP = R1[b0:b0 + 1, 1536:2048]
        RBm = t["A2"][0:1, 14336 + par * 1024: 14848 + par * 1024]
        RBr = t["A2"][0:1, 14848 + par * 1024: 15360 + par * 1024]
        for c in range(NCH):
            cp = c % 2
            xbs = t["xsc"][:, cp * 512:(cp + 1) * 512]
            sqs = t["xsc"][:, 1024 + cp * 512: 1024 + (cp + 1) * 512]
            cv = ctx.ev("scalar", "copy",
                        waits=[("pe", pemk[cp]), ("dve", xsv)],
                        out=xbs, in_=fm(xs, c, t0, t0 + 512))
            sv = ctx.ev("scalar", "activation",
                        out=sqs, in_=xbs, func=AF.Square)
            ctx.mm(pA, pA.t[0:1, 0:512], t["onesb"][:, 0:1], xbs,
                   first=(c == 0), last=(c == NCH - 1), pinc=True,
                   waits=[("act", cv)])
            ctx.mm(pB, pB.t[0:1, 0:512], t["onesb"][:, 0:1], sqs,
                   first=(c == 0), last=(c == NCH - 1), pinc=True,
                   waits=[("act", sv)])
            pemk[cp] = p.cnt["pe"]
        sev = ctx.ev("scalar", "copy", banks=[pA], out=SUM,
                     in_=pA.t[0:1, 0:512])
        sev = ctx.ev("scalar", "copy", banks=[pB], out=VAR,
                     in_=pB.t[0:1, 0:512])
        ctx.ev("vector", "tensor_scalar", waits=[("act", sev)],
               out=MNEG, in0=SUM,
               scalar1=-1.0 / D, scalar2=None, op0=ALU.mult)
        ctx.ev("vector", "tensor_scalar", out=VAR, in0=VAR,
               scalar1=1.0 / D, scalar2=None, op0=ALU.mult)
        ctx.ev("vector", "tensor_tensor", out=TMP, in0=MNEG, in1=MNEG,
               op=ALU.mult)
        dv = ctx.ev("vector", "tensor_tensor", out=VAR, in0=VAR, in1=TMP,
                    op=ALU.subtract)
        ctx.ev("scalar", "activation", waits=[("dve", dv)],
               out=TMP, in_=VAR, func=AF.Ln,
               bias=t["epsc"][b0:b0 + 1, 0:1])
        av = ctx.ev("scalar", "activation", out=VAR, in_=TMP, func=AF.Exp,
                    scale=-0.5)
        ctx.ev("vector", "tensor_copy", out=RBm, in_=MNEG)
        bv = ctx.ev("vector", "tensor_copy", waits=[("act", av)],
                    out=RBr, in_=VAR)
        apv[tau] = (RBm, RBr, bv)

    def apply(tau):
        par = tau % 2
        t0 = tau * 512
        RBm, RBr, bv = apv[tau]
        bm = pS1 if par == 0 else pS2
        rb = pTf_ln if par == 0 else pD_ln
        ctx.mm(bm, bm.t[:, 0:512], t["onesrX"][0:1, :], RBm,
               first=True, last=True, waits=[("dve", bv)])
        ctx.mm(rb, rb.t[:, 0:512], t["onesrX"][0:1, :], RBr,
               first=True, last=True)
        rsb = t["A2"][:, 12288 + par * 512: 12800 + par * 512]
        ctx.ev("vector", "tensor_copy", banks=[rb], out=rsb,
               in_=rb.t[:, 0:512])
        p.wait("vector", "pe", p.cnt["pe"])
        for c in range(NCH):
            tsb = t["SCRF"][:, (c % 2) * 256:(c % 2) * 256 + 256].bitcast(BF16)
            ctx.ev("vector", "tensor_tensor",
                   banks=[bm] if c == NCH - 1 else (),
                   waits=[("pe", bm.ready)],
                   out=tsb, in0=fm(xs, c, t0, t0 + 512),
                   in1=bm.t[:, 0:512], op=ALU.add)
            ctx.ev("vector", "tensor_tensor",
                   out=fm(dst, c, t0, t0 + 512), in0=tsb,
                   in1=rsb, op=ALU.mult)

    pTf_ln = ctx.banks["pTf"]
    pD_ln = ctx.banks["pD"]
    ctx.zv = [0] * NTAU
    for tau in range(NTAU + 1):
        if tau < NTAU:
            stats(tau)
        if tau >= 1:
            apply(tau - 1)
            ctx.zv[tau - 1] = p.cnt["dve"]


def emit_moe_ffn(ctx, w):
    p, t = ctx.p, ctx.t
    xs = t["xs"]
    A1, A2, A3 = t["A1"], t["A2"], t["A3"]
    pA, pB, pC, pD = (ctx.banks[k] for k in ("pA", "pB", "pC", "pD"))
    pS1, pS2 = ctx.banks["pS1"], ctx.banks["pS2"]
    P13 = t["slotP"]
    S2s = t["slotS"]

    def w13_of(e):
        return w["w13"][e] if e < NE else w["sw13"]

    def w2_of(e):
        return w["w2"][e] if e < NE else w["sw2"]

    ctx.load_slice(P13[0], w13_of(0), 0, 256)
    emit_ln(ctx, A1)

    gts = t["gts"]
    ctx.fence("vector")
    lgs = (t["sc8x"], t["sc8y"])
    for tc in range(NTC):
        br = pS1 if tc % 2 == 0 else pS2
        for c in range(NCH):
            ctx.mm(br, br.t[:, 0:NE],
                   fm(A1, c, tc * 128, tc * 128 + 128),
                   t["rwt"][:, c * NE:(c + 1) * NE],
                   first=(c == 0), last=(c == NCH - 1),
                   waits=ctx.zw(tc // 4))
        lg = lgs[tc % 2]
        lv = ctx.ev("scalar", "copy", banks=[br], out=lg,
                    in_=br.t[:, 0:NE])
        mx = t["sc1a"]
        ctx.ev("vector", "reduce_max", waits=[("act", lv)],
               out=mx, in_=lg, axis=AX.X, negate=True)
        ex = t["sc8b"]
        sm = t["sc1b"]
        av = ctx.ev("scalar", "activation", waits=[("dve", p.cnt["dve"])],
                    out=ex, in_=lg, func=AF.Exp, bias=mx[:, 0:1],
                    accum_out=sm[:, 0:1])
        rs = t["sc1a"]
        ctx.ev("vector", "reciprocal", waits=[("act", av)], out=rs, in_=sm)
        sc = t["sc8a"]
        ctx.ev("vector", "tensor_scalar", out=sc, in0=ex,
               scalar1=rs[:, 0:1], scalar2=None, op0=ALU.mult)
        sb = t["sc8z"]
        ctx.ev("vector", "tensor_tensor", out=sb, in0=sc, in1=t["ebbct"],
               op=ALU.add)
        m1 = t["sc1b"]
        ctx.ev("vector", "reduce_max", out=m1, in_=sb, axis=AX.X)
        eq = t["sc8c"]
        ctx.ev("vector", "tensor_scalar", out=eq, in0=sb,
               scalar1=m1[:, 0:1], scalar2=1e30, op0=ALU.is_ge, op1=ALU.mult)
        ctx.ev("vector", "tensor_tensor", out=eq, in0=sb, in1=eq,
               op=ALU.subtract)
        m2 = t["sc1a"]
        ctx.ev("vector", "reduce_max", out=m2, in_=eq, axis=AX.X)
        msk = t["sc8c"]
        ctx.ev("vector", "tensor_scalar", out=msk, in0=sb,
               scalar1=m2[:, 0:1], scalar2=None, op0=ALU.is_ge)
        ctx.ev("vector", "tensor_tensor", out=sc, in0=sc, in1=msk,
               op=ALU.mult)
        ctx.ev("vector", "tensor_scalar", out=gts[:, tc * NE:(tc + 1) * NE],
               in0=sc, scalar1=0.5, scalar2=None, op0=ALU.mult)

    gate_bcs = (A2[:, 0:2048], A2[:, 2048:4096])
    e1s = (A2[:, 8192:8704], A2[:, 8704:9216])
    e3s = (A2[:, 9216:9728], A2[:, 9728:10240])
    grows = (A2[0:1, 14336:16384], A2[0:1, 10240:12288])
    gmark = [0, 0]

    def gates(e):
        par = e % 2
        grow = grows[par]
        gate_bc = gate_bcs[par]
        gdv = p.cnt["dve"]
        for tcq in range(4):
            for k in range(4):
                tc = tcq * 4 + k
                ctx.mm(pS1, pS1.t[0:1, k * 128:(k + 1) * 128],
                       gts[:, tc * NE + e: tc * NE + e + 1],
                       ctx.c_ident, first=True, last=True,
                       pinc=(k == 3), waits=[("dve", gdv)])
            ctx.ev("vector", "tensor_copy", banks=[pS1],
                   out=grow[0:1, tcq * 512:(tcq + 1) * 512],
                   in_=pS1.t[0:1, 0:512])
        gv = p.cnt["dve"]
        for tau in range(NTAU):
            ctx.mm(pS2, pS2.t[:, 0:512], t["onesrX"][0:1, :],
                   grow[0:1, tau * 512:(tau + 1) * 512],
                   first=True, last=True, waits=[("dve", gv)])
            ctx.ev("vector", "tensor_copy", banks=[pS2],
                   waits=[("dve", gmark[par])],
                   out=gate_bc[:, tau * 512:(tau + 1) * 512],
                   in_=pS2.t[:, 0:512])

    gates(0)
    for e in range(NE + 1):
        gate_bc = gate_bcs[e % 2]
        for hc in range(NCH):
            s13 = P13[hc % 2]
            if hc == 4 and e + 1 < NE:
                gates(e + 1)
            if hc + 1 < NCH:
                ctx.load_slice(P13[(hc + 1) % 2], w13_of(e),
                               (hc + 1) * 256, 256)
            elif e < NE:
                ctx.load_slice(P13[0], w13_of(e + 1), 0, 256)
            for tau in range(NTAU):
                t0 = tau * 512
                bh1 = pA if tau % 2 == 0 else pC
                bh3 = pB if tau % 2 == 0 else pD
                for c in range(NCH):
                    ctx.mm(bh1, bh1.t[:, 0:512], s13.tile(c, 0),
                           fm(A1, c, t0, t0 + 512),
                           first=(c == 0), last=(c == NCH - 1), slots=[s13],
                           waits=ctx.zw(tau))
                for c in range(NCH):
                    ctx.mm(bh3, bh3.t[:, 0:512], s13.tile(c, 128),
                           fm(A1, c, t0, t0 + 512),
                           first=(c == 0), last=(c == NCH - 1), slots=[s13],
                           waits=ctx.zw(tau))
                eo = e1s[tau % 2]
                e3 = e3s[tau % 2]
                sv = ctx.ev("scalar", "activation", banks=[bh1],
                            waits=[("dve", p.cnt["dve"])],
                            out=eo, in_=bh1.t[:, 0:512], func=AF.Silu)
                ctx.ev("vector", "tensor_copy", banks=[bh3],
                       out=e3, in_=bh3.t[:, 0:512])
                gdst = A3[:, hc * T + t0: hc * T + t0 + 512]
                ctx.ev("vector", "tensor_tensor", waits=[("act", sv)],
                       out=gdst, in0=eo, in1=e3, op=ALU.mult)
                if e < NE:
                    gmark[e % 2] = ctx.ev(
                        "vector", "tensor_tensor", out=gdst, in0=gdst,
                        in1=gate_bc[:, t0:t0 + 512], op=ALU.mult)
                else:
                    ctx.ev("vector", "tensor_scalar", out=gdst, in0=gdst,
                           scalar1=0.5, scalar2=None, op0=ALU.mult)
        gready = p.cnt["dve"]
        for dc in range(NCH):
            s2 = S2s[dc % 2]
            if dc == 0:
                ctx.load_slice(s2, w2_of(e), 0, 128)
            if dc + 1 < NCH:
                ctx.load_slice(S2s[(dc + 1) % 2], w2_of(e),
                               (dc + 1) * 128, 128)
            for tau in range(NTAU):
                t0 = tau * 512
                by = pS1 if tau % 2 == 0 else pS2
                for hc in range(NCH):
                    ctx.mm(by, by.t[:, 0:512], s2.tile(hc),
                           A3[:, hc * T + t0: hc * T + t0 + 512],
                           first=(hc == 0), last=(hc == NCH - 1),
                           slots=[s2], waits=[("dve", gready)])
                ctx.ev("vector", "tensor_tensor", banks=[by],
                       out=fm(xs, dc, t0, t0 + 512),
                       in0=fm(xs, dc, t0, t0 + 512),
                       in1=by.t[:, 0:512], op=ALU.add)


def emit_l1ffn(ctx, w):
    p, t = ctx.p, ctx.t
    xs = t["xs"]
    A1 = t["A1"]
    pA, pB = ctx.banks["pA"], ctx.banks["pB"]
    S = t["slotS"]
    ctx.load_slice(S[0], w["cbt"], 0, 128)
    emit_ln(ctx, A1)
    for nc_ in range(NCH):
        s0 = S[nc_ % 2]
        if nc_ + 1 < NCH:
            ctx.load_slice(S[(nc_ + 1) % 2], w["cbt"], (nc_ + 1) * 128, 128)
        for tau in range(NTAU):
            t0 = tau * 512
            b = pA if tau % 2 == 0 else pB
            for c in range(NCH):
                ctx.mm(b, b.t[:, 0:512], s0.tile(c),
                       fm(A1, c, t0, t0 + 512),
                       first=(c == 0), last=(c == NCH - 1), slots=[s0],
                       waits=ctx.zw(tau))
            tscr = (t["SCRF"][:, 0:512] if tau % 2 == 0
                    else t["SCRF"][:, 512:1024])
            av = ctx.ev("scalar", "activation", banks=[b],
                        waits=[("dve", p.cnt["dve"])],
                        out=tscr, in_=b.t[:, 0:512], func=AF.Identity,
                        bias=t["l1bh"][:, nc_:nc_ + 1],
                        scale=t["scbh"][:, nc_:nc_ + 1])
            ctx.ev("vector", "tensor_tensor", waits=[("act", av)],
                   out=fm(xs, nc_, t0, t0 + 512),
                   in0=fm(xs, nc_, t0, t0 + 512), in1=tscr, op=ALU.add)


def emit_attn1(ctx, w):
    p, t = ctx.p, ctx.t
    xs = t["xs"]
    A1, A2, A3, A4 = t["A1"], t["A2"], t["A3"], t["A4"]
    pA, pB, pC, pD = (ctx.banks[k] for k in ("pA", "pB", "pC", "pD"))
    pS1, pS2 = ctx.banks["pS1"], ctx.banks["pS2"]
    pTf = ctx.banks["pTf"]
    R1 = t["R1"]
    P13 = t["slotP"]
    S = t["slotS"]

    ctx.load_slice(S[0], w["wv"], 0, 128)
    emit_ln(ctx, A1)

    for jb in range(NCH):
        s0 = S[jb % 2]
        if jb + 1 < NCH:
            ctx.load_slice(S[(jb + 1) % 2], w["wv"], (jb + 1) * 128, 128)
        elif True:
            ctx.load_slice(S[2], w["wo"], 0, 128)
        for tc in range(NTC):
            b = pA if tc % 2 == 0 else pB
            for c in range(NCH):
                ctx.mm(b, b.t[:, 0:128],
                       fm(A1, c, tc * 128, tc * 128 + 128),
                       s0.tile(c),
                       first=(c == 0), last=(c == NCH - 1), slots=[s0],
                       waits=ctx.zw(tc // 4))
            ctx.ev("vector", "tensor_copy", banks=[b],
                   out=tmv(A3, tc, jb * 128, jb * 128 + 128),
                   in_=b.t[:, 0:128])

    qp = A4[:, 0:2048]
    kp = A4[:, 2048:4096]
    Pscr = [A4[:, 4096 + i * 512: 4096 + (i + 1) * 512] for i in range(4)]
    rbrow = (A4[0:1, 6144:6656], A4[64:65, 6144:6656])

    ctx.load_slice(P13[0], w["wqk"], 0, 256)
    for hp in range(NH // 2):
        s13 = P13[hp % 2]
        if hp + 1 < NH // 2:
            ctx.load_slice(P13[(hp + 1) % 2], w["wqk"], (hp + 1) * 256, 256)
        for tau in range(NTAU):
            t0 = tau * 512
            for c in range(NCH):
                ctx.mm(pA, pA.t[:, 0:512], s13.tile(c, 0),
                       fm(A1, c, t0, t0 + 512),
                       first=(c == 0), last=(c == NCH - 1), slots=[s13],
                       waits=ctx.zw(tau))
            ctx.ev("vector", "tensor_copy", banks=[pA],
                   out=qp[:, t0:t0 + 512], in_=pA.t[:, 0:512])
            for c in range(NCH):
                ctx.mm(pB, pB.t[:, 0:512], s13.tile(c, 128),
                       fm(A1, c, t0, t0 + 512),
                       first=(c == 0), last=(c == NCH - 1), slots=[s13],
                       waits=ctx.zw(tau))
            ctx.ev("vector", "tensor_copy", banks=[pB],
                   out=kp[:, t0:t0 + 512], in_=pB.t[:, 0:512])
        qkv = p.cnt["dve"]
        pmark = [0, 0, 0, 0]

        def norm_chain(nqt):
            rbv = [0, 0]
            for h01 in range(2):
                b0 = h01 * 64
                bz = pS1 if h01 == 0 else pS2
                zrow = R1[b0:b0 + 1, 0:512]
                lrow = R1[b0:b0 + 1, 512:1024]
                ctx.ev("vector", "tensor_copy", banks=[bz],
                       out=zrow, in_=bz.t[b0:b0 + 1, 0:512])
                dv = p.cnt["dve"]
                ctx.ev("scalar", "activation", waits=[("dve", dv)],
                       out=lrow, in_=zrow, func=AF.Ln)
                av = ctx.ev("scalar", "activation", out=zrow, in_=lrow,
                            func=AF.Exp, scale=-1.0)
                rbv[h01] = ctx.ev("vector", "tensor_copy",
                                  waits=[("act", av)],
                                  out=rbrow[h01], in_=zrow)
            return rbv

        def norm_apply(nqt, rbv):
            t0 = nqt * 512
            for h01 in range(2):
                b0 = h01 * 64
                bb = pS1 if h01 == 0 else pS2
                ctx.mm(bb, bb.t[b0:b0 + 64, 0:512],
                       t["onesrX"][b0:b0 + 1, 0:64], rbrow[h01],
                       first=True, last=True,
                       tp=(0, b0), waits=[("dve", rbv[h01])])
            for h01 in range(2):
                b0 = h01 * 64
                bb = pS1 if h01 == 0 else pS2
                seg = A2[b0:b0 + 64, hp * T + t0: hp * T + t0 + 512]
                ctx.ev("vector", "tensor_tensor", banks=[bb],
                       out=seg, in0=seg, in1=bb.t[b0:b0 + 64, 0:512],
                       op=ALU.mult)


        pend = None
        for qt in range(NTAU):
            t0 = qt * 512
            ntile = 4 * qt + 4
            def st(h01, Ti):
                b = pA if h01 == 0 else pB
                j = Ti - 4 * qt
                r0 = 128 * j if j > 0 else 0
                lhs = kp[h01 * 64:(h01 + 1) * 64, Ti * 128:(Ti + 1) * 128]
                rhs = qp[h01 * 64:(h01 + 1) * 64, t0 + r0:t0 + 512]
                ctx.mm(b, b.t[:, r0:512], lhs, rhs,
                       first=True, last=(j < 0), waits=[("dve", qkv)])
                if j >= 0:
                    ctx.mm(b, b.t[:, r0:r0 + 128], ctx.c_ident,
                           t["mbig"][:, 0:128],
                           first=False, last=True)

            st(0, 0)
            st(1, 0)
            if pend is not None:
                norm_apply(qt - 1, pend)
                pend = None
            for Ti in range(ntile):
                last_t = (Ti == ntile - 1)
                for h01 in range(2):
                    b = pA if h01 == 0 else pB
                    bc = pC if h01 == 0 else pD
                    bz = pS1 if h01 == 0 else pS2
                    b0 = h01 * 64
                    h = hp * 2 + h01
                    Pz = Pscr[(Ti % 2) * 2 + h01]
                    j = Ti - 4 * qt
                    r0 = 128 * j if j > 0 else 0
                    evx = ctx.ev("scalar", "activation", banks=[b],
                                 waits=[("pe", pmark[(Ti % 2) * 2 + h01])],
                                 out=Pz[:, r0:512], in_=b.t[:, r0:512],
                                 func=AF.Exp, scale=0.125)
                    vsl = tmv(A3, Ti, h * 64, h * 64 + 64)
                    ctx.mm(bc, bc.t[b0:b0 + 64, r0:512],
                           vsl, Pz[:, r0:512], first=(Ti == 0), last=last_t,
                           pinc=False, tp=(0, b0), waits=[("act", evx)])
                    ctx.mm(bz, bz.t[b0:b0 + 1, r0:512],
                           t["onesb"][:, 0:1], Pz[:, r0:512],
                           first=(Ti == 0), last=last_t,
                           pinc=True, tp=(0, b0))
                    pmark[(Ti % 2) * 2 + h01] = p.cnt["pe"]
                    if not last_t:
                        st(h01, Ti + 1)

            for bb in (pC, pD, pS1, pS2):
                bb.ready = p.cnt["pe"]
                bb.started = False
            pend = norm_chain(qt)
            ctx.ev("vector", "tensor_copy", banks=[pC],
                   out=A2[0:64, hp * T + t0: hp * T + t0 + 512],
                   in_=pC.t[0:64, 0:512])
            ctx.ev("vector", "tensor_copy", banks=[pD],
                   out=A2[64:128, hp * T + t0: hp * T + t0 + 512],
                   in_=pD.t[64:128, 0:512])

        norm_apply(NTAU - 1, pend)
    ctx.barrier()
    for jc in range(NCH):
        s0 = S[2 + jc % 2]
        if jc + 1 < NCH:
            ctx.load_slice(S[2 + (jc + 1) % 2], w["wo"], (jc + 1) * 128, 128)
        for tau in range(NTAU):
            t0 = tau * 512
            b = pA if tau % 2 == 0 else pB
            for c in range(NCH):
                ctx.mm(b, b.t[:, 0:512], s0.tile(c),
                       fm(A2, c, t0, t0 + 512),
                       first=(c == 0), last=(c == NCH - 1), slots=[s0])
            tscr = (t["SCRF"][:, 0:512] if tau % 2 == 0
                    else t["SCRF"][:, 512:1024])
            av = ctx.ev("scalar", "activation", banks=[b],
                        waits=[("dve", p.cnt["dve"])],
                        out=tscr, in_=b.t[:, 0:512], func=AF.Identity,
                        bias=t["boc"][:, jc:jc + 1])
            ctx.ev("vector", "tensor_tensor", waits=[("act", av)],
                   out=fm(xs, jc, t0, t0 + 512),
                   in0=fm(xs, jc, t0, t0 + 512), in1=tscr, op=ALU.add)


def emit_attn0(ctx, w):
    p, t = ctx.p, ctx.t
    xs = t["xs"]
    A1, A2, A3, A4 = t["A1"], t["A2"], t["A3"], t["A4"]
    pA, pB, pC, pD = (ctx.banks[k] for k in ("pA", "pB", "pC", "pD"))
    pS1 = ctx.banks["pS1"]
    pT = ctx.banks["pT"]
    pTf = ctx.banks["pTf"]
    S = t["slotS"]
    qkvT = w["qkvT"]

    ctx.load_slice(S[0], qkvT, D, 128)
    phases = int(os.environ.get("ATTN0_PHASES", "9"))
    scanp = int(os.environ.get("ATTN0_SCAN", "9"))
    emit_ln(ctx, A1)
    if phases < 1:
        return

    kmx = t["kmx"]
    k4 = t["k4"]
    for kc in range(NCH):
        s0 = S[kc % 2]
        if kc + 1 < NCH:
            ctx.load_slice(S[(kc + 1) % 2], qkvT, D + (kc + 1) * 128, 128)
        else:
            ctx.load_slice(S[2], qkvT, 0, 128)
        for tau in range(NTAU):
            b = pA if tau % 2 == 0 else pB
            for c in range(NCH):
                ctx.mm(b, b.t[:, 0:512], s0.tile(c),
                       fm(A1, c, tau * 512, tau * 512 + 512),
                       first=(c == 0), last=(c == NCH - 1), slots=[s0],
                       waits=ctx.zw(tau))
            ctx.ev("vector", "reduce_max", banks=[b],
                   out=k4[:, tau:tau + 1], in_=b.t[:, 0:512], axis=AX.X)
        ctx.ev("vector", "reduce_max", out=kmx[:, kc:kc + 1], in_=k4,
               axis=AX.X, negate=True)
    kmxv = p.cnt["dve"]

    S_f = t["SCRF"][:, 512:1024]
    ctx.ev("vector", "memset", ap=S_f, constant=0.0)
    sprev = t["R1"][0:1, 0:D]
    ctx.ev("vector", "memset", ap=sprev, constant=1e-9)

    for hf in range(2):
        hb = hf * 1024
        other = 1 - hf
        sb_ = A2[:, other * 8192: other * 8192 + 8192]
        ketm = sb_[:, 0:1024]
        qh = sb_[:, 1024:2048]
        qhf = sb_[:, 2048:3072]
        am = sb_[:, 3072:3328]
        S_b = sb_[:, 3328:3840]
        if hf == 0:
            ctx.ev("vector", "memset", ap=S_b, constant=0.0)

        if phases < 2:
            continue
        qev = 0
        for jb in range(NCH):
            s0 = S[2 + jb % 2]
            if jb + 1 < NCH:
                ctx.load_slice(S[2 + (jb + 1) % 2], qkvT,
                               (jb + 1) * 128, 128)
            else:
                ctx.load_slice(S[2], qkvT, 2 * D, 128)
            for tc2 in range(NTC // 2):
                tc = hf * 8 + tc2
                b = pA if tc2 % 2 == 0 else pB
                for c in range(NCH):
                    ctx.mm(b, b.t[:, 0:128],
                           fm(A1, c, tc * 128, tc * 128 + 128),
                           s0.tile(c),
                           first=(c == 0), last=(c == NCH - 1), slots=[s0],
                           waits=ctx.zw(tc // 4))
                qe = t["qe"]
                av = ctx.ev("scalar", "activation", banks=[b],
                            waits=[("dve", qev)],
                            out=qe, in_=b.t[:, 0:128], func=AF.Exp)
                ctx.ev("vector", "reduce_sum", waits=[("act", av)],
                       out=t["sc8a"][:, 0:2],
                       in_=qe.rearrange("p (h c) -> p h c", c=HS),
                       axis=AX.X)
                ctx.ev("vector", "reciprocal", out=t["sc8b"][:, 0:2],
                       in_=t["sc8a"][:, 0:2])
                ctx.ev("vector", "tensor_scalar", out=t["sc8b"][:, 0:2],
                       in0=t["sc8b"][:, 0:2], scalar1=0.125, scalar2=None,
                       op0=ALU.mult)
                for h2 in range(2):
                    qev = ctx.ev("vector", "tensor_scalar",
                                 out=tmv(A2, tc, jb * 128 + h2 * 64,
                                         jb * 128 + h2 * 64 + 64),
                                 in0=qe[:, h2 * 64:(h2 + 1) * 64],
                                 scalar1=t["sc8b"][:, h2:h2 + 1],
                                 scalar2=None, op0=ALU.mult)

        if phases < 3:
            continue
        for kc in range(NCH):
            s0 = S[kc % 2]
            if kc == 0:
                ctx.load_slice(s0, qkvT, D, 128)
            if kc + 1 < NCH:
                ctx.load_slice(S[(kc + 1) % 2], qkvT, D + (kc + 1) * 128, 128)
            for tau2 in range(2):
                b = pA if tau2 % 2 == 0 else pB
                tb = hb + tau2 * 512
                for c in range(NCH):
                    ctx.mm(b, b.t[:, 0:512], s0.tile(c),
                           fm(A1, c, tb, tb + 512),
                           first=(c == 0), last=(c == NCH - 1), slots=[s0],
                           waits=ctx.zw(tb // 512))
                ctx.ev("scalar", "activation", banks=[b],
                       waits=[("dve", kmxv)],
                       out=fm(A3, kc, tb, tb + 512), in_=b.t[:, 0:512],
                       func=AF.Exp, bias=kmx[:, kc:kc + 1])

        if phases < 4:
            continue
        for jb in range(NCH):
            s0 = S[2 + jb % 2]
            if jb + 1 < NCH:
                ctx.load_slice(S[2 + (jb + 1) % 2], qkvT,
                               2 * D + (jb + 1) * 128, 128)
            for tc2 in range(NTC // 2):
                tc = hf * 8 + tc2
                b = pA if tc2 % 2 == 0 else pB
                for c in range(NCH):
                    ctx.mm(b, b.t[:, 0:128],
                           fm(A1, c, tc * 128, tc * 128 + 128),
                           s0.tile(c),
                           first=(c == 0), last=(c == NCH - 1), slots=[s0],
                           waits=ctx.zw(tc // 4))
                ctx.ev("vector", "tensor_copy", banks=[b],
                       out=A4[:, tc2 * D + jb * 128: tc2 * D + jb * 128 + 128],
                       in_=b.t[:, 0:128])
        ctx.barrier()

        if phases < 5:
            continue
        kexp_act = p.cnt["act"]
        qsm_dve = p.cnt["dve"]
        qtrans_prev = 0
        for ck2 in range(8):
            ck = hf * 8 + ck2
            cb = ck * 128
            for c in range(NCH):
                ctx.trans(pT, pT.t[:, c * 128:(c + 1) * 128],
                          fm(A3, c, cb, cb + 128),
                          first=(c == 0), last=(c == NCH - 1),
                          waits=[("act", kexp_act)])
            ctx.ev("vector", "tensor_copy", banks=[pT], out=ketm,
                   in_=pT.t[:, 0:1024])
            kv = p.cnt["dve"]
            if scanp < 1:
                continue
            for half in range(2):
                b = pA if half == 0 else pB
                ctx.mm(b, b.t[:, 0:512], t["lte2"][:, 0:128],
                       ketm[:, half * 512:(half + 1) * 512],
                       first=True, last=False, waits=[("dve", kv)])
                ctx.mm(b, b.t[:, 0:512], t["onesrf"][0:1, :],
                       sprev[0:1, half * 512:(half + 1) * 512],
                       first=False, last=True)
            for half in range(2):
                b = pA if half == 0 else pB
                lnz = t["R1"][:, 1024:1536]
                rz = t["R1"][:, 1536:2048]
                ctx.ev("scalar", "activation", banks=[b],
                       waits=[("dve", p.cnt["dve"])], out=lnz,
                       in_=b.t[:, 0:512], func=AF.Ln)
                av = ctx.ev("scalar", "activation", out=rz, in_=lnz,
                            func=AF.Exp, scale=-1.0)
                ctx.ev("vector", "tensor_tensor",
                       waits=[("act", av), ("pe", qtrans_prev)],
                       out=qh[:, half * 512:(half + 1) * 512],
                       in0=tmv(A2, ck, half * 512, half * 512 + 512),
                       in1=rz, op=ALU.mult)
            qv = p.cnt["dve"]
            for c in range(NCH):
                ctx.trans(pT, pT.t[:, c * 128:(c + 1) * 128],
                          qh[:, c * 128:(c + 1) * 128],
                          first=(c == 0), last=(c == NCH - 1),
                          waits=[("dve", qv)])
            ctx.ev("vector", "tensor_copy", banks=[pT], out=qhf,
                   in_=pT.t[:, 0:1024])
            qfv = p.cnt["dve"]
            qtrans_prev = pT.ready
            pS2 = ctx.banks["pS2"]
            for half in range(2):
                bcs = pS1 if half == 0 else pS2
                ctx.mm(bcs, bcs.t[0:1, 0:512], t["onesb"][:, 0:1],
                       ketm[:, half * 512:(half + 1) * 512],
                       first=True, last=True, waits=[("dve", kv)])
            svmark = p.cnt["dve"]
            svmark_a = p.cnt["act"]
            if scanp < 3:
                continue
            for hp2 in range(NH // 2):
                he, ho = hp2 * 2, hp2 * 2 + 1
                for h01, h in ((0, he), (1, ho)):
                    ba = pA if h01 == 0 else pB
                    lhs = A3[(h % 2) * 64:(h % 2) * 64 + 64,
                             (h // 2) * T + cb:(h // 2) * T + cb + 128]
                    rhs = qhf[(h % 2) * 64:(h % 2) * 64 + 64,
                              (h // 2) * 128:(h // 2) * 128 + 128]
                    ctx.mm(ba, ba.t[:, 0:128], lhs, rhs,
                           first=True, last=True,
                           waits=[("dve", qfv)])
                ctx.ev("vector", "tensor_tensor", banks=[pA],
                       out=am[:, 0:128], in0=pA.t[:, 0:128],
                       in1=t["lte2"][:, 0:128], op=ALU.mult)
                mv = ctx.ev("vector", "tensor_tensor", banks=[pB],
                            out=am[:, 128:256], in0=pB.t[:, 0:128],
                            in1=t["lte2"][:, 128:256], op=ALU.mult)
                if scanp < 4:
                    continue
                for h01, h in ((0, he), (1, ho)):
                    b0 = h01 * 64
                    ctx.mm(pD, pD.t[b0:b0 + 64, 0:128],
                           S_b[b0:b0 + 64, hp2 * 64:(hp2 + 1) * 64],
                           qhf[b0:b0 + 64,
                               (h // 2) * 128:(h // 2) * 128 + 128],
                           first=True, last=False, pinc=False, tp=(0, b0),
                           waits=[("dve", svmark), ("act", svmark_a)])
                    ctx.mm(pD, pD.t[b0:b0 + 64, 0:128],
                           A4[:, ck2 * D + h * 64: ck2 * D + (h + 1) * 64],
                           am[:, h01 * 128:(h01 + 1) * 128],
                           first=False, last=True, pinc=(h01 == 1),
                           tp=(0, b0), waits=[("dve", mv)])
                for h01, h in ((0, he), (1, ho)):
                    b0 = h01 * 64
                    ctx.mm(pTf, pTf.t[b0:b0 + 64, 0:64],
                           ketm[:, h * 64:(h + 1) * 64],
                           A4[:, ck2 * D + h * 64: ck2 * D + (h + 1) * 64],
                           first=True, last=True, pinc=(h01 == 1),
                           tp=(0, b0))
                ctx.ev("scalar", "copy", banks=[pD],
                       out=A1[:, hp2 * T + cb: hp2 * T + cb + 128],
                       in_=pD.t[:, 0:128])
                if scanp < 5:
                    continue
                ctx.ev("vector", "tensor_tensor", banks=[pTf],
                       out=S_f[:, hp2 * 64:(hp2 + 1) * 64],
                       in0=S_f[:, hp2 * 64:(hp2 + 1) * 64],
                       in1=pTf.t[:, 0:64], op=ALU.add)
                ctx.ev("scalar", "copy", waits=[("dve", p.cnt["dve"])],
                       out=S_b[:, hp2 * 64:(hp2 + 1) * 64],
                       in_=S_f[:, hp2 * 64:(hp2 + 1) * 64])
            for half in range(2):
                bcs = pS1 if half == 0 else pS2
                ctx.ev("vector", "tensor_tensor", banks=[bcs],
                       out=sprev[0:1, half * 512:(half + 1) * 512],
                       in0=sprev[0:1, half * 512:(half + 1) * 512],
                       in1=bcs.t[0:1, 0:512], op=ALU.add)
        ctx.barrier()

    if phases < 6:
        return
    ctx.load_slice(S[0], w["outT"], 0, 128)
    for jc in range(NCH):
        s0 = S[jc % 2]
        if jc + 1 < NCH:
            ctx.load_slice(S[(jc + 1) % 2], w["outT"], (jc + 1) * 128, 128)
        for tau in range(NTAU):
            t0 = tau * 512
            b = (pA, pB, pC, pD)[tau]
            for c in range(NCH):
                ctx.mm(b, b.t[:, 0:512], s0.tile(c),
                       fm(A1, c, t0, t0 + 512),
                       first=(c == 0), last=(c == NCH - 1), slots=[s0])
            ctx.ev("vector", "tensor_tensor", banks=[b],
                   out=fm(xs, jc, t0, t0 + 512),
                   in0=fm(xs, jc, t0, t0 + 512),
                   in1=b.t[:, 0:512], op=ALU.add)


class BlockShim:
    def __init__(self, blk):
        self.blk = blk

    def __getattr__(self, name):
        dec = getattr(self.blk, name)

        def call(fn):
            dec(fn)
        return call


def build_program(subids):
    nc = bass.Bass("TRN2", target_bir_lowering=False)
    h = {}
    h["xin"] = nc.dram_tensor("xin", [NSEQ, D, T], F32,
                              kind="ExternalInput").ap()
    h["xout"] = nc.dram_tensor("xout", [NSEQ, D, T], F32,
                               kind="ExternalOutput").ap()

    def din(name, shape, dt=F32):
        h[name] = nc.dram_tensor(name, shape, dt, kind="ExternalInput").ap()

    needs = set(subids)
    if needs & {FFN0A, FFN0B}:
        din("w13", [NE, D, 2 * D]); din("w2", [NE, D, D])
        din("sw13", [D, 2 * D]); din("sw2", [D, D])
        din("rw", [D, NE]); din("ebbc", [128, NE])
    if ATTN0 in needs:
        din("wqkvT", [D, 3 * D]); din("woutT", [D, D])
    if ATTN1 in needs:
        din("wqk", [D, 2 * D]); din("wv", [D, D]); din("wo", [D, D])
        din("bo", [D])
    if needs & {FFN1A, FFN1B}:
        din("cbt", [D, D]); din("scb", [D]); din("l1b", [D])
    din("c_ident", [128, 128], BF16)
    din("c_lte2", [128, 256], BF16)
    din("c_mbig", [128, 128], BF16)

    prog = Prog()
    with ExitStack() as st:
        e = st.enter_context
        t = {}

        def sb(name, shape, dt):
            t[name] = e(nc.sbuf_tensor(name, shape, dt))[
                tuple(slice(None) for _ in shape)]

        sb("xs", [128, NCH * T], F32)
        sb("A1", [128, NCH * T], BF16)
        sb("A2", [128, NCH * T], BF16)
        sb("A3", [128, NCH * T], BF16)
        sb("A4", [128, 8192], BF16)
        sb("wsl", [128, 8192], BF16)
        sb("R1", [128, T], F32)
        sb("SCRF", [128, 1024], F32)
        sb("rwt", [128, NCH * NE], BF16)
        sb("ebbct", [128, NE], F32)
        sb("kmx", [128, NCH], F32)
        sb("k4", [128, 4], F32)
        for nm in ("sc8a", "sc8b", "sc8c", "sc8x", "sc8y", "sc8z"):
            sb(nm, [128, NE], F32)
        for nm in ("sc1a", "sc1b", "epsc"):
            sb(nm, [128, 1], F32)
        sb("boc", [128, NCH], F32)
        sb("scbh", [128, NCH], F32)
        sb("l1bh", [128, NCH], F32)
        sb("onesb", [128, 1], BF16)
        sb("onesrX", [128, 128], BF16)
        sb("onesrf", [1, 128], F32)
        sb("identb", [128, 128], BF16)
        sb("lte2", [128, 256], BF16)
        sb("mbig", [128, 128], BF16)

        t["gts"] = t["A4"][:, 0:NTC * NE]
        t["qe"] = t["A4"][:, 256:384]
        t["xsc"] = t["A4"][:, 2048:4096]

        banks = {}
        pab = e(nc.psum_tensor("pAB", [128, 1024], F32))[:, :]
        t["pAB"] = pab
        banks["pA"] = Bank(pab[:, 0:512])
        banks["pB"] = Bank(pab[:, 512:1024])
        for nm in ("pC", "pD", "pS1", "pS2"):
            banks[nm] = Bank(e(nc.psum_tensor(nm, [128, 512], F32))[:, :])
        banks["pT"] = Bank(e(nc.psum_tensor("pT", [128, 1024], BF16))[:, :])
        banks["pTf"] = Bank(e(nc.psum_tensor("pTf", [128, 512], F32))[:, :])

        sems = {nm: e(nc.semaphore("s_" + nm))
                for nm in ("pe", "dve", "act", "g", "h",
                           "g0", "g1", "g2", "g3", "g4", "g5")}

        ctx = Ctx(nc, prog)
        ctx.t = t
        ctx.banks = banks
        ctx.c_ident = t["identb"]
        t["slotP"] = [Slot(t["wsl"][:, 0:2048], 256, "g0"),
                      Slot(t["wsl"][:, 2048:4096], 256, "g1")]
        t["slotS"] = [Slot(t["wsl"][:, 4096 + i * 1024:
                                    4096 + (i + 1) * 1024], 128,
                           "g%d" % (2 + i))
                      for i in range(4)]

        p = prog
        p.dma("sync", t["identb"], h["c_ident"][:, :], "h")
        p.dma("sync", t["lte2"], h["c_lte2"][:, :], "h")
        p.dma("sync", t["mbig"], h["c_mbig"][:, :], "h")
        if needs & {FFN0A, FFN0B}:
            p.dma("gpsimd", t["rwt"].rearrange("p (c e) -> p c e", c=NCH),
                  h["rw"].rearrange("(c p) e -> p c e", p=128), "g")
            p.dma("sync", t["ebbct"], h["ebbc"][:, :], "h")
        if ATTN1 in needs:
            bor = h["bo"].rearrange("(c p one) -> c p one", p=128, one=1)
            for c in range(NCH):
                p.dma("sync", t["boc"][:, c:c + 1], bor[c], "h")
        if needs & {FFN1A, FFN1B}:
            scbr = h["scb"].rearrange("(c p one) -> c p one", p=128, one=1)
            l1br = h["l1b"].rearrange("(c p one) -> c p one", p=128, one=1)
            for c in range(NCH):
                p.dma("sync", t["scbh"][:, c:c + 1], scbr[c], "h")
                p.dma("sync", t["l1bh"][:, c:c + 1], l1br[c], "h")
        hload = p.cnt["h"]
        gload = p.cnt["g"]
        for eng in ("vector", "scalar", "tensor"):
            p.wait(eng, "h", hload)
            p.wait(eng, "g", gload)
        prog.op("vector", "memset", ap=t["onesb"], constant=1.0)
        prog.op("vector", "memset", ap=t["epsc"], constant=1e-5)
        prog.op("vector", "memset", ap=t["onesrX"], constant=1.0)
        if needs & {FFN1A, FFN1B}:
            prog.op("vector", "tensor_scalar", out=t["scbh"], in0=t["scbh"],
                    scalar1=1.0 / 254.0, scalar2=None, op0=ALU.mult)
            prog.op("vector", "tensor_scalar", out=t["l1bh"], in0=t["l1bh"],
                    scalar1=0.5, scalar2=None, op0=ALU.mult)
        mv = prog.op("vector", "memset", ap=t["onesrf"], constant=1.0,
                     inc=("dve", 1))
        for eng in ("scalar", "tensor"):
            p.wait(eng, "dve", mv)
        p.wait("gpsimd", "dve", mv)

        for seq in range(NSEQ):
            xr = h["xin"].rearrange("s (c p) t -> s c p t", p=128)
            p.wait("sync", "dve", p.cnt["dve"])
            p.wait("sync", "pe", p.cnt["pe"])
            p.wait("sync", "h", p.cnt["h"])
            for c in range(NCH):
                p.dma("sync", fm(t["xs"], c, 0, T), xr[seq, c], "h")
            hx = p.cnt["h"]
            for eng in ("vector", "scalar", "tensor"):
                p.wait(eng, "h", hx)
            for sid in subids:
                kind = EMITTERS[sid]
                if kind == "ffn0":
                    emit_moe_ffn(ctx, {
                        "w13": [h["w13"][ei] for ei in range(NE)],
                        "w2": [h["w2"][ei] for ei in range(NE)],
                        "sw13": h["sw13"], "sw2": h["sw2"]})
                elif kind == "ffn1":
                    emit_l1ffn(ctx, {"cbt": h["cbt"]})
                elif kind == "attn0":
                    emit_attn0(ctx, {"qkvT": h["wqkvT"], "outT": h["woutT"]})
                elif kind == "attn1":
                    emit_attn1(ctx, {"wqk": h["wqk"], "wv": h["wv"],
                                     "wo": h["wo"]})
                ctx.barrier()
            xo = h["xout"].rearrange("s (c p) t -> s c p t", p=128)
            p.wait("sync", "dve", p.cnt["dve"])
            p.wait("sync", "pe", p.cnt["pe"])
            for c in range(NCH):
                p.dma("sync", xo[seq, c], fm(t["xs"], c, 0, T), "h")
        p.wait("sync", "h", p.cnt["h"])

        blk = e(nc.Block())
        prog.replay(BlockShim(blk), sems)
    return nc


def _consts():
    ident = np.eye(128, dtype=np.float32).astype(BFNP)
    lte = (np.arange(128)[:, None] <= np.arange(128)[None, :])
    lte2 = np.concatenate([lte, lte], axis=1).astype(np.float32).astype(BFNP)
    s = np.arange(128)[:, None]
    u = np.arange(128)[None, :]
    mbig = np.where(s <= u, 0.0, -1e9).astype(np.float32).astype(BFNP)
    return ident, lte2, mbig


def _interleave128(a, b):
    R, N = a.shape
    return np.concatenate([a.reshape(R, N // 128, 128),
                           b.reshape(R, N // 128, 128)],
                          axis=2).reshape(R, 2 * N)


def host_inputs(inputs):
    ident, lte2, mbig = _consts()

    def f32(a):
        return np.ascontiguousarray(np.asarray(a, dtype=np.float32))

    w1 = np.asarray(inputs["L0_ffn_w1"], dtype=np.float32)
    w3 = np.asarray(inputs["L0_ffn_w3"], dtype=np.float32)
    w13 = np.ascontiguousarray(
        np.stack([_interleave128(w1[ei], w3[ei]) for ei in range(NE)]))
    sw13 = np.ascontiguousarray(_interleave128(
        np.asarray(inputs["L0_ffn_sw1"], dtype=np.float32),
        np.asarray(inputs["L0_ffn_sw3"], dtype=np.float32)))
    wqk = np.ascontiguousarray(_interleave128(
        np.asarray(inputs["L1_attn_Wq"], dtype=np.float32),
        np.asarray(inputs["L1_attn_Wk"], dtype=np.float32)))
    common = {
        "w13": w13,
        "w2": f32(inputs["L0_ffn_w2"]),
        "sw13": sw13,
        "sw2": f32(inputs["L0_ffn_sw2"]),
        "rw": f32(inputs["L0_ffn_router_w"]),
        "ebbc": np.ascontiguousarray(np.tile(
            np.asarray(inputs["L0_ffn_expert_bias"],
                       dtype=np.float32)[None, :], (128, 1))),
        "wqkvT": f32(np.asarray(inputs["L0_attn_W_qkv"]).T),
        "woutT": f32(np.asarray(inputs["L0_attn_W_out"]).T),
        "wqk": wqk,
        "wv": f32(inputs["L1_attn_Wv"]),
        "wo": f32(inputs["L1_attn_Wo"]),
        "bo": f32(inputs["L1_attn_bo"]),
        "cbt": f32(np.asarray(inputs["L1_ffn_CB"]).T),
        "scb": f32(inputs["L1_ffn_SCB"]),
        "l1b": f32(inputs["L1_ffn_bias"]),
        "c_ident": ident, "c_lte2": lte2, "c_mbig": mbig,
    }
    return common


USED = {
    "ffn0": ["w13", "w2", "sw13", "sw2", "rw", "ebbc"],
    "attn0": ["wqkvT", "woutT"],
    "attn1": ["wqk", "wv", "wo", "bo"],
    "ffn1": ["cbt", "scb", "l1b"],
}


def kernel(**inputs):
    x = np.asarray(inputs["L0_attn_x"], dtype=np.float32)
    B = x.shape[0]
    percore = B // 8
    xcur = np.ascontiguousarray(
        x.reshape(8, percore, T, D).transpose(0, 1, 3, 2))
    common = host_inputs(inputs)
    for group in LAUNCH_GROUPS:
        names = {"c_ident", "c_lte2", "c_mbig"}
        for sid in group:
            names.update(USED[EMITTERS[sid]])
        nc = build_program(group)
        in_maps = [dict({n: common[n] for n in names}, xin=xcur[c])
                   for c in range(8)]
        res = run_bass_kernel_spmd(nc, in_maps, core_ids=list(range(8)))
        xcur = np.stack([res.results[c]["xout"] for c in range(8)])
    out = np.ascontiguousarray(
        xcur.transpose(0, 1, 3, 2).reshape(B, T, D)).astype(np.float32)
    return out
